# version 29
# baseline (speedup 1.0000x reference)
import contextlib
import numpy as np
import ml_dtypes
import concourse.bass as bass
import concourse.mybir as mybir
from concourse.bass_utils import run_bass_kernel_spmd

F32 = mybir.dt.float32
BF16 = mybir.dt.bfloat16
I8 = mybir.dt.int8
AL = mybir.AluOpType
AF = mybir.ActivationFunctionType
AX = mybir.AxisListType

S, D, NST, HID = 2048, 1024, 32, 3072
TC = 512
NTC = S // TC
KC = D // 128
HC = HID // 128
NSL = TC // 128
INV127 = float(1.0 / 127.0)
INV127SQ = float(1.0 / (127.0 * 127.0))
LN_EPS = 1e-5

COMPUTE = ("tensor", "vector", "scalar", "gpsimd")
BODY_OF = {"tensor": "tensor", "vector": "vector", "scalar": "scalar",
           "gpsimd": "gpsimd", "qsp": "sync", "qpool": "gpsimd", "qact": "scalar"}


class Sched:

    def __init__(self):
        self.recs = []
        self.scount = {}
        self.last_w = {}
        self.readers = {}
        self.known = {}
        self.unwritten = set()
        self.defer = None

    def emit(self, eng, fn, reads=(), writes=(), sync_reads=(), dkey=None):
        if self.defer is not None:
            self.defer.append((eng, fn, reads, writes, sync_reads, dkey))
            return
        body = BODY_OF[eng]
        skey = dkey if dkey is not None else eng
        inc = 16 if dkey is not None else 1
        myval = self.scount.get(skey, 0) + inc
        self.scount[skey] = myval
        kn = self.known.setdefault(body, {})
        waits = {}

        def need(dep, force=False):
            dk, dv = dep
            if dk == eng and dkey is None and not force:
                return
            if kn.get(dk, 0) >= dv:
                return
            if waits.get(dk, 0) < dv:
                waits[dk] = dv

        for t in sync_reads:
            if t in self.last_w:
                need(self.last_w[t], force=True)
            else:
                self.unwritten.add(t)
        for t in reads:
            if t in self.last_w:
                need(self.last_w[t])
            else:
                self.unwritten.add(t)
        for t in writes:
            if t in self.last_w:
                need(self.last_w[t])
            for dk, dv in self.readers.get(t, {}).items():
                need((dk, dv))
        for dk, dv in waits.items():
            kn[dk] = dv
        for t in writes:
            self.last_w[t] = (skey, myval)
            self.readers[t] = {}
        for t in list(reads) + list(sync_reads):
            rd = self.readers.setdefault(t, {})
            if rd.get(skey, 0) < myval:
                rd[skey] = myval
        self.recs.append((eng, fn, waits, skey, inc))


def build_program():
    nc = bass.Bass("TRN2", target_bir_lowering=False)

    x_d = nc.dram_tensor("x", [S, D], F32, kind="ExternalInput").ap()
    dtw_d = nc.dram_tensor("dtw", [128, 8192], F32, kind="ExternalInput").ap()
    bw_d = nc.dram_tensor("bw", [128, 256], F32, kind="ExternalInput").ap()
    cw_d = nc.dram_tensor("cw", [128, 256], F32, kind="ExternalInput").ap()
    bb_d = nc.dram_tensor("bb", [32, 1], F32, kind="ExternalInput").ap()
    cb_d = nc.dram_tensor("cb", [32, 1], F32, kind="ExternalInput").ap()
    dtb_d = nc.dram_tensor("dtb", [128, 8], F32, kind="ExternalInput").ap()
    dsk_d = nc.dram_tensor("dsk", [128, 8], F32, kind="ExternalInput").ap()
    logA_d = nc.dram_tensor("logA", [128, 256], F32, kind="ExternalInput").ap()
    wg8_d = nc.dram_tensor("wg8", [HID, D], I8, kind="ExternalInput").ap()
    wu8_d = nc.dram_tensor("wu8", [HID, D], I8, kind="ExternalInput").ap()
    wd8_d = nc.dram_tensor("wd8", [D, HID], I8, kind="ExternalInput").ap()
    identb_d = nc.dram_tensor("identb_in", [128, 128], BF16, kind="ExternalInput").ap()
    out_d = nc.dram_tensor("out", [S, D], F32, kind="ExternalOutput").ap()
    bscr = [nc.dram_tensor(f"bscr{v}", [32, TC], BF16, kind="Internal").ap() for v in range(2)]
    cscr = [nc.dram_tensor(f"cscr{v}", [32, TC], BF16, kind="Internal").ap() for v in range(2)]

    es = contextlib.ExitStack()
    with es:
        def sb(name, shape, dtype):
            return es.enter_context(nc.sbuf_tensor(name, shape, dtype))

        def ps(name, shape, dtype):
            return es.enter_context(nc.psum_tensor(name, shape, dtype))

        bwb = sb("bwb", [128, 256], BF16)
        cwb = sb("cwb", [128, 256], BF16)
        bbs = sb("bbs", [32, 1], F32)
        cbs = sb("cbs", [32, 1], F32)
        dtbs = sb("dtbs", [128, 8], F32)
        dsks = sb("dsks", [128, 8], F32)
        acol = sb("acol", [128, 256], F32)
        identb = sb("identb", [128, 128], BF16)
        constp = sb("constp", [128, 12], F32)
        sx = sb("sx", [128, 1], F32)
        sx2 = sb("sx2", [128, 1], F32)
        mvec = sb("mvec", [128, 1], F32)
        msq = sb("msq", [128, 1], F32)
        var = sb("var", [128, 1], F32)
        lnv = sb("lnv", [128, 1], F32)
        rstd = sb("rstd", [128, 1], F32)
        negmr = sb("negmr", [128, 1], F32)
        x_tm = [sb(f"x_tm{i}", [128, 1024], F32) for i in range(2)]
        x_re = [sb(f"x_re{i}", [128, 1024], F32) for i in range(2)]
        xn_tm = [sb(f"xn_tm{i}", [128, 1024], BF16) for i in range(NSL)]
        xn_fm = [[sb(f"xn_fm{v}_{c}", [128, TC], BF16) for c in range(KC)] for v in range(2)]
        dt_fm = [[sb(f"dt_fm{v}_{c}", [128, TC], BF16) for c in range(KC)] for v in range(2)]
        ub = [[sb(f"ub{v}_{c}", [128, TC], BF16) for c in range(KC)] for v in range(2)]
        sp_r = sb("sp_r", [128, TC], BF16)
        sp_a = sb("sp_a", [128, TC], F32)
        sp_e = sb("sp_e", [128, TC], BF16)
        b_sb = sb("b_sb", [32, TC], BF16)
        c_sb = sb("c_sb", [32, TC], BF16)
        dtw_buf = [sb(f"dtw_buf{j}", [128, 1024], BF16) for j in range(2)]
        wgbuf = [sb(f"wgbuf{j}", [128, 1024], BF16) for j in range(2)]
        wubuf = [sb(f"wubuf{j}", [128, 1024], BF16) for j in range(2)]
        wdhalf = [sb(f"wdhalf{j}", [128, 1536], BF16) for j in range(2)]
        sg_b = [sb(f"sg_b{j}", [128, TC], BF16) for j in range(1)]
        up_b = [sb(f"up_b{j}", [128, TC], BF16) for j in range(1)]
        h_act = [sb(f"h_act{h}", [128, TC], BF16) for h in range(HC)]
        g_fm = [[sb(f"g_fm{v}_{c}", [128, TC], BF16) for c in range(KC)] for v in range(2)]
        btp = [sb(f"btp{j}", [128, 2, TC], BF16) for j in range(2)]
        ctp = [sb(f"ctp{j}", [128, 2, TC], BF16) for j in range(2)]
        apair = [sb(f"apair{j}", [128, 2, TC + 1], F32) for j in range(2)]
        wpair = [sb(f"wpair{j}", [128, 2, TC + 1], BF16) for j in range(3)]
        hpair = [sb(f"hpair{j}", [128, 2, TC + 1], BF16) for j in range(3)]
        hcp = [sb(f"hcp{j}", [128, 2, TC], BF16) for j in range(1)]
        e_b = sb("e_b", [128, TC], BF16)
        ones_t = sb("ones_t", [128, TC], BF16)
        pairc = [sb(f"pairc{c}", [128, TC], BF16) for c in range(KC)]
        quadb = [sb(f"quadb{j}", [128, TC], BF16) for j in range(1)]
        y_acc = [sb(f"y_acc{c}", [128, TC], F32) for c in range(KC)]
        carry = sb("carry", [128, 256], F32)
        out_tm = [sb(f"out_tm{j}", [128, 1024], F32) for j in range(2)]

        PS0 = ps("PS0", [128, TC], F32)
        PS1 = ps("PS1", [128, TC], F32)
        PS2 = ps("PS2", [128, TC], F32)
        PS3 = ps("PS3", [128, TC], F32)
        PSB = ps("PSB", [32, TC], F32)
        PSC = ps("PSC", [32, TC], F32)
        PSBF = ps("PSBF", [128, TC], BF16)
        PSBF2 = ps("PSBF2", [128, TC], BF16)

        sems = {}
        block = es.enter_context(nc.Block())

        sch = Sched()
        E = sch.emit
        T_SPR = ("stage", "sp_r")
        T_SPA = ("stage", "sp_a")
        T_SPE = ("stage", "sp_e")

        E("gpsimd", lambda en: en.memset(carry[:], 0.0),
          reads=(), writes=tuple(f"carry_{k}" for k in range(256)))
        E("gpsimd", lambda en: en.memset(ones_t[:], 1.0), reads=(), writes=("ones_t",))
        for _aj in range(2):
            E("gpsimd", (lambda aj=_aj: lambda en: en.memset(apair[aj][:], 0.0))(),
              reads=(), writes=(f"apair{_aj}_0", f"apair{_aj}_1"))
        _cvals = [LN_EPS, -0.5, -1.0, 1.0, INV127, INV127SQ, 1.0 / D, INV127 / 2, INV127SQ / 2]
        for _ci, _cv in enumerate(_cvals):
            E("gpsimd", (lambda ci_=_ci, cv_=_cv: lambda en: en.memset(
                constp[:, ci_:ci_ + 1], cv_))(), reads=(), writes=("constp",))
        E("qpool", lambda en: en.dma_start(out=bwb[:], in_=bw_d),
          reads=(), writes=("bwb",), dkey="qbwb")
        E("qpool", lambda en: en.dma_start(out=cwb[:], in_=cw_d),
          reads=(), writes=("cwb",), dkey="qcwb")
        E("qsp", lambda en: en.dma_start(out=sp_a[:, 0:256], in_=logA_d),
          reads=(), writes=(T_SPA,), dkey="qsp_a")
        E("scalar", lambda en: en.activation(out=sp_r[:, 0:256], in_=sp_a[:, 0:256], func=AF.Exp),
          reads=(T_SPA,), writes=(T_SPR,))
        E("vector", lambda en: en.tensor_scalar_mul(out=acol[:], in0=sp_r[:, 0:256], scalar1=constp[:, 2:3]),
          reads=(T_SPR, "constp"), writes=("acol",))
        E("qsp", lambda en: en.dma_start(out=bbs[:], in_=bb_d), reads=(), writes=("bbs",), dkey="qbbs")
        E("qsp", lambda en: en.dma_start(out=cbs[:], in_=cb_d), reads=(), writes=("cbs",), dkey="qcbs")
        E("qsp", lambda en: en.dma_start(out=dtbs[:], in_=dtb_d), reads=(), writes=("dtbs",), dkey="qdtbs")
        E("qsp", lambda en: en.dma_start(out=dsks[:], in_=dsk_d), reads=(), writes=("dsks",), dkey="qdsks")
        E("qsp", lambda en: en.dma_start(out=identb[:], in_=identb_d), reads=(), writes=("identb",), dkey="qident")

        def emitA(tc):
            v = tc % 2
            t0 = tc * TC
            for i in range(2):
                E("qsp", (lambda i_=i, t0_=t0: lambda en: en.dma_start(
                    out=x_tm[i_][:], in_=x_d[t0_ + i_ * 128: t0_ + (i_ + 1) * 128, :]))(),
                  reads=(), writes=(f"x_tm{i}",), dkey=f"qx{i}")
            for i in range(NSL):
                xi = x_tm[i % 2]
                xn_i = xn_tm[i]
                E("vector", (lambda xi_=xi: lambda en: en.reduce_sum(
                    out=sx[:], in_=xi_[:], axis=AX.X))(),
                  reads=(f"x_tm{i % 2}",), writes=("sx",))
                E("scalar", (lambda xi_=xi, xn_=xn_i: lambda en: en.activation(
                    out=xn_[:], in_=xi_[:], func=AF.Square, accum_out=sx2[:]))(),
                  reads=(f"x_tm{i % 2}",), writes=(f"xn_tm{i}", "sx2"))
                E("vector", lambda en: en.tensor_scalar_mul(out=mvec[:], in0=sx[:], scalar1=constp[:, 6:7]),
                  reads=("sx", "constp"), writes=("mvec",))
                E("vector", lambda en: en.tensor_mul(msq[:], mvec[:], mvec[:]),
                  reads=(), sync_reads=("mvec",), writes=("msq",))
                E("vector", lambda en: en.tensor_scalar(
                    out=var[:], in0=sx2[:], scalar1=constp[:, 6:7], scalar2=msq[:],
                    op0=AL.mult, op1=AL.subtract),
                  reads=("sx2", "constp"), sync_reads=("msq",), writes=("var",))
                E("scalar", lambda en: en.activation(out=lnv[:], in_=var[:], func=AF.Ln, bias=constp[:, 0:1]),
                  reads=("var", "constp"), writes=("lnv",))
                E("scalar", lambda en: en.activation(out=rstd[:], in_=lnv[:], func=AF.Exp, scale=constp[:, 1:2]),
                  reads=("lnv", "constp"), writes=("rstd",))
                E("vector", lambda en: en.tensor_scalar(
                    out=negmr[:], in0=mvec[:], scalar1=rstd[:], scalar2=constp[:, 2:3],
                    op0=AL.mult, op1=AL.mult),
                  reads=("rstd", "constp"), sync_reads=("mvec",), writes=("negmr",))
                E("scalar", (lambda xi_=xi, xn_=xn_i: lambda en: en.activation(
                    out=xn_[:], in_=xi_[:], func=AF.Identity, scale=rstd[:], bias=negmr[:]))(),
                  reads=(f"x_tm{i % 2}", "negmr"), sync_reads=("rstd",), writes=(f"xn_tm{i}",))
                if i + 2 < NSL:
                    E("qsp", (lambda i_=i, t0_=t0: lambda en: en.dma_start(
                        out=x_tm[(i_ + 2) % 2][:],
                        in_=x_d[t0_ + (i_ + 2) * 128: t0_ + (i_ + 3) * 128, :]))(),
                      reads=(), writes=(f"x_tm{(i + 2) % 2}",), dkey=f"qx{(i + 2) % 2}")
            for c in range(KC):
                for i in range(NSL):
                    E("tensor", (lambda c_=c, i_=i: lambda en: en.transpose(
                        out=PSBF[:, i_ * 128:(i_ + 1) * 128],
                        in_=xn_tm[i_][:, c_ * 128:(c_ + 1) * 128], identity=identb[:]))(),
                      reads=(f"xn_tm{i}", "identb"), writes=("psbf",))
                E("scalar", (lambda c_=c, v_=v: lambda en: en.copy(out=xn_fm[v_][c_][:], in_=PSBF[:]))(),
                  reads=("psbf",), writes=(f"xn_fm{v}_{c}",))

        def emitB(tc):
            v = tc % 2
            E("qpool", lambda en: en.dma_start(out=dtw_buf[0][:], in_=dtw_d[:, 0:1024]),
              reads=(), writes=("dtw_buf0",), dkey="qdtw0")
            for ec in range(KC):
                j = ec % 2
                if ec + 1 < KC:
                    jn = (ec + 1) % 2
                    E("qpool", (lambda ec_=ec, jn_=jn: lambda en: en.dma_start(
                        out=dtw_buf[jn_][:], in_=dtw_d[:, (ec_ + 1) * 1024:(ec_ + 2) * 1024]))(),
                      reads=(), writes=(f"dtw_buf{jn}",), dkey=f"qdtw{jn}")
                pdst, ptok = (PS0, "ps0") if ec % 2 == 0 else (PS3, "ps3")
                for kc in range(KC):
                    E("tensor", (lambda j_=j, kc_=kc, pd_=pdst: lambda en: en.matmul(
                        out=pd_[:], lhsT=dtw_buf[j_][:, kc_ * 128:(kc_ + 1) * 128],
                        rhs=xn_fm[v][kc_][:], start=(kc_ == 0), stop=(kc_ == KC - 1)))(),
                      reads=(f"dtw_buf{j}", f"xn_fm{v}_{kc}", ptok), writes=(ptok,))
                E("scalar", (lambda ec_=ec, pd_=pdst: lambda en: en.activation(
                    out=sp_r[:], in_=pd_[:], func=AF.Relu, bias=dtbs[:, ec_:ec_ + 1]))(),
                  reads=(ptok, "dtbs"), writes=(T_SPR,))
                E("scalar", (lambda ec_=ec, pd_=pdst: lambda en: en.activation(
                    out=sp_a[:], in_=pd_[:], func=AF.Abs, bias=dtbs[:, ec_:ec_ + 1]))(),
                  reads=(ptok, "dtbs"), writes=(T_SPA,))
                E("scalar", lambda en: en.activation(out=sp_e[:], in_=sp_a[:], func=AF.Exp, scale=constp[:, 2:3]),
                  reads=(T_SPA, "constp"), writes=(T_SPE,))
                E("scalar", lambda en: en.activation(out=sp_a[:], in_=sp_e[:], func=AF.Ln, bias=constp[:, 3:4]),
                  reads=(T_SPE, "constp"), writes=(T_SPA,))
                E("vector", (lambda ec_=ec, v_=v: lambda en: en.tensor_add(
                    dt_fm[v_][ec_][:], sp_r[:], sp_a[:]))(),
                  reads=(T_SPR, T_SPA), writes=(f"dt_fm{v}_{ec}",))
                E("vector", (lambda ec_=ec, v_=v: lambda en: en.tensor_mul(
                    ub[v_][ec_][:], dt_fm[v_][ec_][:], xn_fm[v_][ec_][:]))(),
                  reads=(f"dt_fm{v}_{ec}", f"xn_fm{v}_{ec}"), writes=(f"ub{v}_{ec}",))
            for kc in range(KC):
                E("tensor", (lambda kc_=kc, v_=v: lambda en: en.matmul(
                    out=PSB[:], lhsT=bwb[:, kc_ * 32:(kc_ + 1) * 32], rhs=xn_fm[v_][kc_][:],
                    start=(kc_ == 0), stop=(kc_ == KC - 1)))(),
                  reads=("bwb", f"xn_fm{v}_{kc}", "psb"), writes=("psb",))
            E("scalar", lambda en: en.activation(out=b_sb[:], in_=PSB[:], func=AF.Identity, bias=bbs[:]),
              reads=("psb", "bbs"), writes=("b_sb",))
            for kc in range(KC):
                E("tensor", (lambda kc_=kc, v_=v: lambda en: en.matmul(
                    out=PSC[:], lhsT=cwb[:, kc_ * 32:(kc_ + 1) * 32], rhs=xn_fm[v_][kc_][:],
                    start=(kc_ == 0), stop=(kc_ == KC - 1)))(),
                  reads=("cwb", f"xn_fm{v}_{kc}", "psc"), writes=("psc",))
            E("scalar", lambda en: en.activation(out=c_sb[:], in_=PSC[:], func=AF.Identity, bias=cbs[:]),
              reads=("psc", "cbs"), writes=("c_sb",))
            E("qpool", (lambda v_=v: lambda en: en.dma_start(out=bscr[v_], in_=b_sb[:]))(),
              reads=("b_sb",), writes=(f"bscr{v}",), dkey="qbscr")
            E("qpool", (lambda v_=v: lambda en: en.dma_start(out=cscr[v_], in_=c_sb[:]))(),
              reads=("c_sb",), writes=(f"cscr{v}",), dkey="qcscr")

        def emitByinit(tc):
            v = tc % 2
            for ec in range(KC):
                E("vector", (lambda ec_=ec, v_=v: lambda en: en.tensor_scalar_mul(
                    out=y_acc[ec_][:], in0=xn_fm[v_][ec_][:], scalar1=dsks[:, ec_:ec_ + 1]))(),
                  reads=(f"xn_fm{v}_{ec}", "dsks"), writes=(f"y_acc{ec}",))

        def emitC_deferred(tc):
            sch.defer = []
            emitC(tc)
            out = sch.defer
            sch.defer = None
            return out

        def emitC(tc):
            v = tc % 2

            def load_gu(hc_):
                j = hc_ % 2
                E("qpool", (lambda h_=hc_, j_=j: lambda en: en.dma_start(
                    out=wgbuf[j_][:], in_=wg8_d[h_ * 128:(h_ + 1) * 128, :]))(),
                  reads=(), writes=(f"wgbuf{j}",), dkey=f"qwg{j}")
                E("qpool", (lambda h_=hc_, j_=j: lambda en: en.dma_start(
                    out=wubuf[j_][:], in_=wu8_d[h_ * 128:(h_ + 1) * 128, :]))(),
                  reads=(), writes=(f"wubuf{j}",), dkey=f"qwu{j}")

            load_gu(0)
            for hc in range(HC):
                j = hc % 2
                if hc + 1 < HC:
                    load_gu(hc + 1)
                for kc in range(KC):
                    E("tensor", (lambda j_=j, kc_=kc, v_=v: lambda en: en.matmul(
                        out=PS1[:], lhsT=wgbuf[j_][:, kc_ * 128:(kc_ + 1) * 128],
                        rhs=xn_fm[v_][kc_][:], start=(kc_ == 0), stop=(kc_ == KC - 1)))(),
                      reads=(f"wgbuf{j}", f"xn_fm{v}_{kc}", "ps1"), writes=("ps1",))
                for kc in range(KC):
                    E("tensor", (lambda j_=j, kc_=kc, v_=v: lambda en: en.matmul(
                        out=PS2[:], lhsT=wubuf[j_][:, kc_ * 128:(kc_ + 1) * 128],
                        rhs=xn_fm[v_][kc_][:], start=(kc_ == 0), stop=(kc_ == KC - 1)))(),
                      reads=(f"wubuf{j}", f"xn_fm{v}_{kc}", "ps2"), writes=("ps2",))
                E("scalar", lambda en: en.activation(
                    out=e_b[:], in_=PS1[:], func=AF.Tanh, scale=constp[:, 7:8]),
                  reads=("ps1", "constp"), writes=("e_b",))
                E("scalar", lambda en: en.activation(
                    out=sg_b[0][:], in_=PS1[:], func=AF.Identity, scale=constp[:, 7:8]),
                  reads=("ps1", "constp"), writes=("sg_b0",))
                E("scalar", (lambda j_=j: lambda en: en.copy(
                    out=up_b[0][:], in_=PS2[:]))(),
                  reads=("ps2",), writes=("up_b0",))
                E("gpsimd", lambda en: en.tensor_add(e_b[:], e_b[:], ones_t[:]),
                  reads=("ones_t",), sync_reads=("e_b",), writes=("e_b",))
                E("gpsimd", (lambda h_=hc: lambda en: en.tensor_mul(
                    h_act[h_][:], sg_b[0][:], up_b[0][:]))(),
                  reads=("sg_b0", "up_b0"), writes=(f"h_act{hc}",))
                E("gpsimd", (lambda h_=hc: lambda en: en.tensor_mul(
                    h_act[h_][:], h_act[h_][:], e_b[:]))(),
                  reads=("e_b",), sync_reads=(f"h_act{hc}",), writes=(f"h_act{hc}",))
            for c in range(KC):
                pdst, ptok = (PS1, "ps1") if c % 2 == 0 else (PS2, "ps2")
                for half in range(2):
                    j = (2 * c + half) % 2
                    E("qpool", (lambda c_=c, hf_=half, j_=j: lambda en: en.dma_start(
                        out=wdhalf[j_][:],
                        in_=wd8_d[c_ * 128:(c_ + 1) * 128, hf_ * 1536:(hf_ + 1) * 1536]))(),
                      reads=(), writes=(f"wdhalf{j}",), dkey=f"qwd{j}")
                for half in range(2):
                    j = (2 * c + half) % 2
                    for h12 in range(12):
                        h = half * 12 + h12
                        E("tensor", (lambda j_=j, h12_=h12, h_=h, pd_=pdst: lambda en: en.matmul(
                            out=pd_[:], lhsT=wdhalf[j_][:, h12_ * 128:(h12_ + 1) * 128],
                            rhs=h_act[h_][:], start=(h_ == 0), stop=(h_ == HC - 1)))(),
                          reads=(f"wdhalf{j}", f"h_act{h}", ptok), writes=(ptok,))
                E("scalar", (lambda pd_=pdst: lambda en: en.activation(
                    out=e_b[:], in_=pd_[:], func=AF.Tanh, scale=constp[:, 8:9]))(),
                  reads=(ptok, "constp"), writes=("e_b",))
                E("scalar", (lambda c_=c, pd_=pdst, v_=v: lambda en: en.activation(
                    out=g_fm[v_][c_][:], in_=pd_[:], func=AF.Identity, scale=constp[:, 8:9]))(),
                  reads=(ptok, "constp"), writes=(f"g_fm{v}_{c}",))
                E("gpsimd", lambda en: en.tensor_add(e_b[:], e_b[:], ones_t[:]),
                  reads=("ones_t",), sync_reads=("e_b",), writes=("e_b",))
                E("gpsimd", (lambda c_=c, v_=v: lambda en: en.tensor_mul(
                    g_fm[v_][c_][:], g_fm[v_][c_][:], e_b[:]))(),
                  reads=("e_b",), sync_reads=(f"g_fm{v}_{c}",), writes=(f"g_fm{v}_{c}",))

        def emitD(tc, cdefer=()):
            cq = list(cdefer)
            ci = 0

            def pump(k):
                nonlocal ci
                for _ in range(k):
                    if ci < len(cq):
                        e_, f_, r_, w_, s_, d_ = cq[ci]
                        ci += 1
                        sch.emit(e_, f_, r_, w_, s_, d_)

            v = tc % 2
            t0e = tc * TC
            for i in range(2):
                E("qsp", (lambda i_=i, t0_=t0e: lambda en: en.dma_start(
                    out=x_re[i_][:], in_=x_d[t0_ + i_ * 128: t0_ + (i_ + 1) * 128, :]))(),
                  reads=(), writes=(f"x_re{i}",), dkey=f"qxr{i}")

            def bcast(np2):
                j2 = np2 % 2
                for half in range(2):
                    n_ = 2 * np2 + half
                    E("qsp", (lambda n2=n_, j_=j2, h_=half, v_=v: lambda en: en.dma_start(
                        out=btp[j_][:, h_, :], in_=bscr[v_][n2:n2 + 1, :].partition_broadcast(128)))(),
                      reads=(f"bscr{v}",), writes=(f"btp{j2}_{half}",), dkey=f"qbt{2 * j2 + half}")
                    E("qsp", (lambda n2=n_, j_=j2, h_=half, v_=v: lambda en: en.dma_start(
                        out=ctp[j_][:, h_, :], in_=cscr[v_][n2:n2 + 1, :].partition_broadcast(128)))(),
                      reads=(f"cscr{v}",), writes=(f"ctp{j2}_{half}",), dkey=f"qct{2 * j2 + half}")

            bcast(0)
            kctr = 0
            pairctr = 0
            for np_ in range(NST // 2):
                n0, n1 = 2 * np_, 2 * np_ + 1
                j2 = np_ % 2
                if np_ + 1 < NST // 2:
                    bcast(np_ + 1)
                for c in range(KC):
                    kp = pairctr % 3
                    hj = 0
                    ap_ = pairctr % 2
                    pairctr += 1
                    for (n_, half) in ((n0, 0), (n1, 1)):
                        E("scalar", (lambda c_=c, n2=n_, a_=ap_, h_=half, v_=v: lambda en: en.activation(
                            out=apair[a_][:, h_, 1:TC + 1], in_=dt_fm[v_][c_][:], func=AF.Exp,
                            scale=acol[:, c_ * 32 + n2: c_ * 32 + n2 + 1]))(),
                          reads=(f"dt_fm{v}_{c}", "acol"), writes=(f"apair{ap_}_{half}",))
                    for (n_, half) in ((n0, 0), (n1, 1)):
                        slot = c * 32 + n_
                        E("scalar", (lambda k_=kp, h_=half, sl_=slot: lambda en: en.copy(
                            out=wpair[k_][:, h_, 0:1], in_=carry[:, sl_:sl_ + 1]))(),
                          reads=(f"carry_{slot}",), writes=(f"wpair{kp}_inj",))
                    E("vector", (lambda c_=c, j_=j2, k_=kp, v_=v: lambda en: en.tensor_mul(
                        wpair[k_][:, :, 1:TC + 1],
                        ub[v_][c_][:].rearrange("p (o t) -> p o t", o=1).broadcast_to([128, 2, TC]),
                        btp[j_][:]))(),
                      reads=(f"ub{v}_{c}", f"btp{j2}_0", f"btp{j2}_1"), writes=(f"wpair{kp}",))
                    E("vector", (lambda a_=ap_, kp_=kp: lambda en: en.tensor_tensor_scan(
                        out=hpair[kp_][:].rearrange("p a t -> p (a t)"),
                        data0=apair[a_][:].rearrange("p a t -> p (a t)"),
                        data1=wpair[kp_][:].rearrange("p a t -> p (a t)"),
                        initial=0.0, op0=AL.mult, op1=AL.add))(),
                      reads=(f"apair{ap_}_0", f"apair{ap_}_1", f"wpair{kp}", f"wpair{kp}_inj"),
                      writes=(f"hpair{kp}_0", f"hpair{kp}_1"))
                    if tc + 1 < NTC:
                        for (n_, half) in ((n0, 0), (n1, 1)):
                            slot = c * 32 + n_
                            E("gpsimd", (lambda kp_=kp, h_=half, sl_=slot: lambda en: en.tensor_copy(
                                out=carry[:, sl_:sl_ + 1], in_=hpair[kp_][:, h_, TC:TC + 1]))(),
                              reads=(f"hpair{kp}_{half}",), writes=(f"carry_{slot}",))
                    E("vector", (lambda j_=j2, kp_=kp, hj_=hj: lambda en: en.tensor_mul(
                        hcp[hj_][:], hpair[kp_][:, :, 1:TC + 1], ctp[j_][:]))(),
                      reads=(f"hpair{kp}_0", f"hpair{kp}_1", f"ctp{j2}_0", f"ctp{j2}_1"),
                      writes=(f"hcp{hj}",))
                    pump(6)
                    if np_ % 2 == 0:
                        E("vector", (lambda c_=c, hj_=hj: lambda en: en.tensor_add(
                            pairc[c_][:], hcp[hj_][:, 0, :], hcp[hj_][:, 1, :]))(),
                          reads=(f"hcp{hj}",), writes=(f"pairc{c}",))
                    else:
                        E("vector", (lambda hj_=hj: lambda en: en.tensor_add(
                            quadb[0][:], hcp[hj_][:, 0, :], hcp[hj_][:, 1, :]))(),
                          reads=(f"hcp{hj}",), writes=("quadb0",))
                        E("vector", (lambda c_=c: lambda en: en.tensor_add(
                            quadb[0][:], pairc[c_][:], quadb[0][:]))(),
                          reads=(f"pairc{c}", "quadb0"), writes=("quadb0",))
                        E("vector", (lambda c_=c: lambda en: en.tensor_add(
                            y_acc[c_][:], y_acc[c_][:], quadb[0][:]))(),
                          reads=(f"y_acc{c}", "quadb0"), writes=(f"y_acc{c}",))
            pump(10 ** 9)

        def emitE(tc):
            v = tc % 2
            t0 = tc * TC
            for c in range(KC):
                E("vector", (lambda c_=c, v_=v: lambda en: en.tensor_mul(
                    ub[v_][c_][:], y_acc[c_][:], g_fm[v_][c_][:]))(),
                  reads=(f"y_acc{c}", f"g_fm{v}_{c}"), writes=(f"ub{v}_{c}",))
            for i in range(NSL):
                jo = i % 2
                for ch in range(2):
                    pb, pbtok = (PSBF, "psbf") if (2 * i + ch) % 2 == 0 else (PSBF2, "psbf2")
                    for c4 in range(4):
                        c = ch * 4 + c4
                        E("tensor", (lambda c_=c, i_=i, c4_=c4, pb_=pb, v_=v: lambda en: en.transpose(
                            out=pb_[:, c4_ * 128:(c4_ + 1) * 128],
                            in_=ub[v_][c_][:, i_ * 128:(i_ + 1) * 128], identity=identb[:]))(),
                          reads=(f"ub{v}_{c}", "identb"), writes=(pbtok,))
                    E("vector", (lambda i_=i, ch_=ch, jo_=jo, pb_=pb: lambda en: en.tensor_add(
                        out_tm[jo_][:, ch_ * 512:(ch_ + 1) * 512], pb_[:],
                        x_re[i_ % 2][:, ch_ * 512:(ch_ + 1) * 512]))(),
                      reads=(pbtok, f"x_re{i % 2}"), writes=(f"out_tm{jo}",))
                E("qsp", (lambda i_=i, jo_=jo, t0_=t0: lambda en: en.dma_start(
                    out=out_d[t0_ + i_ * 128: t0_ + (i_ + 1) * 128, :], in_=out_tm[jo_][:]))(),
                  reads=(f"out_tm{jo}",), writes=(f"outdma{tc}_{i}",), dkey=f"qout{jo}")
                if i + 2 < NSL:
                    E("qsp", (lambda i_=i, t0_=t0: lambda en: en.dma_start(
                        out=x_re[(i_ + 2) % 2][:],
                        in_=x_d[t0_ + (i_ + 2) * 128: t0_ + (i_ + 3) * 128, :]))(),
                      reads=(), writes=(f"x_re{(i + 2) % 2}",), dkey=f"qxr{(i + 2) % 2}")

        emitA(0)
        emitB(0)
        emitByinit(0)
        for tc in range(NTC):
            if tc + 1 < NTC:
                emitA(tc + 1)
                emitB(tc + 1)
            emitD(tc, emitC_deferred(tc))
            emitE(tc)
            if tc + 1 < NTC:
                emitByinit(tc + 1)

        for (eng_, fn_, w_, skey_, inc_) in sch.recs:
            assert not (eng_ in ("qsp", "qpool", "qact") and skey_ == eng_), "DMA record missing dkey"
        for sk in sch.scount:
            sems[sk] = es.enter_context(nc.semaphore(("s_" + str(sk))[:30]))

        def build_body(body_engs):
            def run(eng_obj):
                for (eng, fn, waits, skey, inc) in sch.recs:
                    if eng not in body_engs:
                        continue
                    ws = sorted(waits.items(), key=lambda kv: str(kv[0]))
                    for de, dv in ws[:-1]:
                        eng_obj.wait_ge(sems[de], dv)
                    ins = fn(eng_obj)
                    if ws:
                        ins = ins._wait_ge(sems[ws[-1][0]], ws[-1][1])
                    ins.then_inc(sems[skey], inc)
            return run

        block.tensor(build_body(("tensor",)))
        block.vector(build_body(("vector",)))
        block.scalar(build_body(("scalar", "qact")))
        block.gpsimd(build_body(("gpsimd", "qpool")))
        out_keys = [k for k in sch.scount if str(k).startswith("qout")]

        def sync_body(eng_obj):
            build_body(("qsp",))(eng_obj)
            for k in out_keys:
                eng_obj.wait_ge(sems[k], sch.scount[k])
        block.sync(sync_body)

    return nc


_CACHE = {}


def _get_nc(debug=False):
    key = ("nc", debug)
    if key not in _CACHE:
        _CACHE[key] = build_program()
    return _CACHE[key]


def kernel(L0_attn_x, L0_attn_log_A, L0_attn_B_w, L0_attn_B_b, L0_attn_C_w,
           L0_attn_C_b, L0_attn_D_skip, L0_attn_dt_w, L0_attn_dt_b,
           L0_ffn_w_gate, L0_ffn_w_up, L0_ffn_w_down):
    nc = _get_nc()
    f32 = np.float32
    x = np.ascontiguousarray(np.asarray(L0_attn_x, dtype=f32))
    dt_w = np.ascontiguousarray(np.asarray(L0_attn_dt_w, dtype=f32))
    B_w = np.ascontiguousarray(np.asarray(L0_attn_B_w, dtype=f32))
    C_w = np.ascontiguousarray(np.asarray(L0_attn_C_w, dtype=f32))
    log_A = np.ascontiguousarray(np.asarray(L0_attn_log_A, dtype=f32))

    dtw = np.ascontiguousarray(
        dt_w.reshape(8, 128, 8, 128).transpose(3, 0, 2, 1)).reshape(128, 8192)
    bw = np.ascontiguousarray(B_w.reshape(32, 8, 128).transpose(2, 1, 0)).reshape(128, 256)
    cw = np.ascontiguousarray(C_w.reshape(32, 8, 128).transpose(2, 1, 0)).reshape(128, 256)
    logA = np.ascontiguousarray(log_A.reshape(8, 128, 32).transpose(1, 0, 2)).reshape(128, 256)
    bb = np.ascontiguousarray(np.asarray(L0_attn_B_b, dtype=f32).reshape(32, 1))
    cb = np.ascontiguousarray(np.asarray(L0_attn_C_b, dtype=f32).reshape(32, 1))
    dtb = np.ascontiguousarray(np.asarray(L0_attn_dt_b, dtype=f32).reshape(8, 128).T)
    dsk = np.ascontiguousarray(np.asarray(L0_attn_D_skip, dtype=f32).reshape(8, 128).T)

    def int8_view(w):
        w = np.ascontiguousarray(np.asarray(w, dtype=np.int32))
        return w.view(np.int8).reshape(w.shape[0], w.shape[1], 4)[:, :, 0]

    wg8 = np.ascontiguousarray(
        int8_view(L0_ffn_w_gate).reshape(24, 128, 8, 128).transpose(0, 3, 2, 1)).reshape(HID, D)
    wu8 = np.ascontiguousarray(
        int8_view(L0_ffn_w_up).reshape(24, 128, 8, 128).transpose(0, 3, 2, 1)).reshape(HID, D)
    wd8 = np.ascontiguousarray(
        int8_view(L0_ffn_w_down).reshape(8, 128, 24, 128).transpose(0, 3, 2, 1)).reshape(D, HID)
    identb = np.eye(128, dtype=ml_dtypes.bfloat16)

    common = dict(dtw=dtw, bw=bw, cw=cw, bb=bb, cb=cb, dtb=dtb, dsk=dsk,
                  logA=logA, wg8=wg8, wu8=wu8, wd8=wd8, identb_in=identb)
    in_maps = [dict(common, x=np.ascontiguousarray(x[k])) for k in range(8)]
    res = run_bass_kernel_spmd(nc, in_maps, core_ids=list(range(8)))
    out = np.stack([np.asarray(r["out"], dtype=f32) for r in res.results])
    return out


# version 30
# speedup vs baseline: 1.1026x; 1.1026x over previous
import contextlib
import numpy as np
import ml_dtypes
import concourse.bass as bass
import concourse.mybir as mybir
from concourse.bass_utils import run_bass_kernel_spmd

F32 = mybir.dt.float32
BF16 = mybir.dt.bfloat16
I8 = mybir.dt.int8
AL = mybir.AluOpType
AF = mybir.ActivationFunctionType
AX = mybir.AxisListType

S, D, NST, HID = 2048, 1024, 32, 3072
TC = 512
NTC = S // TC
KC = D // 128
HC = HID // 128
NSL = TC // 128
INV127 = float(1.0 / 127.0)
INV127SQ = float(1.0 / (127.0 * 127.0))
LN_EPS = 1e-5

COMPUTE = ("tensor", "vector", "scalar", "gpsimd")
BODY_OF = {"tensor": "tensor", "vector": "vector", "scalar": "scalar",
           "gpsimd": "gpsimd", "qsp": "sync", "qpool": "gpsimd", "qact": "scalar"}


class Sched:

    def __init__(self):
        self.recs = []
        self.scount = {}
        self.last_w = {}
        self.readers = {}
        self.known = {}
        self.unwritten = set()
        self.defer = None

    def emit(self, eng, fn, reads=(), writes=(), sync_reads=(), dkey=None):
        if self.defer is not None:
            self.defer.append((eng, fn, reads, writes, sync_reads, dkey))
            return
        body = BODY_OF[eng]
        skey = dkey if dkey is not None else eng
        inc = 16 if dkey is not None else 1
        myval = self.scount.get(skey, 0) + inc
        self.scount[skey] = myval
        kn = self.known.setdefault(body, {})
        waits = {}

        def need(dep, force=False):
            dk, dv = dep
            if dk == eng and dkey is None and not force:
                return
            if kn.get(dk, 0) >= dv:
                return
            if waits.get(dk, 0) < dv:
                waits[dk] = dv

        for t in sync_reads:
            if t in self.last_w:
                need(self.last_w[t], force=True)
            else:
                self.unwritten.add(t)
        for t in reads:
            if t in self.last_w:
                need(self.last_w[t])
            else:
                self.unwritten.add(t)
        for t in writes:
            if t in self.last_w:
                need(self.last_w[t])
            for dk, dv in self.readers.get(t, {}).items():
                need((dk, dv))
        for dk, dv in waits.items():
            kn[dk] = dv
        for t in writes:
            self.last_w[t] = (skey, myval)
            self.readers[t] = {}
        for t in list(reads) + list(sync_reads):
            rd = self.readers.setdefault(t, {})
            if rd.get(skey, 0) < myval:
                rd[skey] = myval
        self.recs.append((eng, fn, waits, skey, inc))


def build_program():
    nc = bass.Bass("TRN2", target_bir_lowering=False)

    x_d = nc.dram_tensor("x", [S, D], F32, kind="ExternalInput").ap()
    dtw_d = nc.dram_tensor("dtw", [128, 8192], F32, kind="ExternalInput").ap()
    bw_d = nc.dram_tensor("bw", [128, 256], F32, kind="ExternalInput").ap()
    cw_d = nc.dram_tensor("cw", [128, 256], F32, kind="ExternalInput").ap()
    bb_d = nc.dram_tensor("bb", [32, 1], F32, kind="ExternalInput").ap()
    cb_d = nc.dram_tensor("cb", [32, 1], F32, kind="ExternalInput").ap()
    dtb_d = nc.dram_tensor("dtb", [128, 8], F32, kind="ExternalInput").ap()
    dsk_d = nc.dram_tensor("dsk", [128, 8], F32, kind="ExternalInput").ap()
    logA_d = nc.dram_tensor("logA", [128, 256], F32, kind="ExternalInput").ap()
    wg8_d = nc.dram_tensor("wg8", [HID, D], I8, kind="ExternalInput").ap()
    wu8_d = nc.dram_tensor("wu8", [HID, D], I8, kind="ExternalInput").ap()
    wd8_d = nc.dram_tensor("wd8", [D, HID], I8, kind="ExternalInput").ap()
    identb_d = nc.dram_tensor("identb_in", [128, 128], BF16, kind="ExternalInput").ap()
    out_d = nc.dram_tensor("out", [S, D], F32, kind="ExternalOutput").ap()
    bscr = [nc.dram_tensor(f"bscr{v}", [32, TC], BF16, kind="Internal").ap() for v in range(2)]
    cscr = [nc.dram_tensor(f"cscr{v}", [32, TC], BF16, kind="Internal").ap() for v in range(2)]

    es = contextlib.ExitStack()
    with es:
        def sb(name, shape, dtype):
            return es.enter_context(nc.sbuf_tensor(name, shape, dtype))

        def ps(name, shape, dtype):
            return es.enter_context(nc.psum_tensor(name, shape, dtype))

        bwb = sb("bwb", [128, 256], BF16)
        cwb = sb("cwb", [128, 256], BF16)
        bbs = sb("bbs", [32, 1], F32)
        cbs = sb("cbs", [32, 1], F32)
        dtbs = sb("dtbs", [128, 8], F32)
        dsks = sb("dsks", [128, 8], F32)
        acol = sb("acol", [128, 256], F32)
        identb = sb("identb", [128, 128], BF16)
        constp = sb("constp", [128, 8], F32)
        sx = sb("sx", [128, 1], F32)
        sx2 = sb("sx2", [128, 1], F32)
        mvec = sb("mvec", [128, 1], F32)
        msq = sb("msq", [128, 1], F32)
        var = sb("var", [128, 1], F32)
        lnv = sb("lnv", [128, 1], F32)
        rstd = sb("rstd", [128, 1], F32)
        negmr = sb("negmr", [128, 1], F32)
        x_tm = [sb(f"x_tm{i}", [128, 1024], F32) for i in range(2)]
        x_re = [sb(f"x_re{i}", [128, 1024], F32) for i in range(2)]
        xn_tm = [sb(f"xn_tm{i}", [128, 1024], BF16) for i in range(NSL)]
        xn_fm = [[sb(f"xn_fm{v}_{c}", [128, TC], BF16) for c in range(KC)] for v in range(2)]
        dt_fm = [[sb(f"dt_fm{v}_{c}", [128, TC], BF16) for c in range(KC)] for v in range(2)]
        ub = [[sb(f"ub{v}_{c}", [128, TC], BF16) for c in range(KC)] for v in range(2)]
        sp_r = sb("sp_r", [128, TC], BF16)
        sp_a = sb("sp_a", [128, TC], F32)
        sp_e = sb("sp_e", [128, TC], BF16)
        b_sb = sb("b_sb", [32, TC], BF16)
        c_sb = sb("c_sb", [32, TC], BF16)
        dtw_buf = [sb(f"dtw_buf{j}", [128, 1024], BF16) for j in range(2)]
        wgbuf = [sb(f"wgbuf{j}", [128, 1024], BF16) for j in range(2)]
        wubuf = [sb(f"wubuf{j}", [128, 1024], BF16) for j in range(2)]
        wdhalf = [sb(f"wdhalf{j}", [128, 1536], BF16) for j in range(2)]
        sg_b = [sb(f"sg_b{j}", [128, TC], BF16) for j in range(2)]
        up_b = [sb(f"up_b{j}", [128, TC], BF16) for j in range(2)]
        h_act = [sb(f"h_act{h}", [128, TC], BF16) for h in range(HC)]
        g_fm = [[sb(f"g_fm{v}_{c}", [128, TC], BF16) for c in range(KC)] for v in range(2)]
        btp = [sb(f"btp{j}", [128, 2, TC], BF16) for j in range(2)]
        ctp = [sb(f"ctp{j}", [128, 2, TC], BF16) for j in range(2)]
        apair = [sb(f"apair{j}", [128, 2, TC + 1], F32) for j in range(2)]
        wpair = [sb(f"wpair{j}", [128, 2, TC + 1], BF16) for j in range(3)]
        hpair = [sb(f"hpair{j}", [128, 2, TC + 1], BF16) for j in range(3)]
        hcp = [sb(f"hcp{j}", [128, 2, TC], BF16) for j in range(2)]
        pairc = [sb(f"pairc{c}", [128, TC], BF16) for c in range(KC)]
        quadb = [sb(f"quadb{j}", [128, TC], BF16) for j in range(1)]
        y_acc = [sb(f"y_acc{c}", [128, TC], F32) for c in range(KC)]
        carry = sb("carry", [128, 256], F32)
        out_tm = [sb(f"out_tm{j}", [128, 1024], F32) for j in range(1)]

        PS0 = ps("PS0", [128, TC], F32)
        PS1 = ps("PS1", [128, TC], F32)
        PS2 = ps("PS2", [128, TC], F32)
        PS3 = ps("PS3", [128, TC], F32)
        PSB = ps("PSB", [32, TC], F32)
        PSC = ps("PSC", [32, TC], F32)
        PSBF = ps("PSBF", [128, TC], BF16)
        PSBF2 = ps("PSBF2", [128, TC], BF16)

        sems = {}
        block = es.enter_context(nc.Block())

        sch = Sched()
        E = sch.emit
        T_SPR = ("stage", "sp_r")
        T_SPA = ("stage", "sp_a")
        T_SPE = ("stage", "sp_e")

        E("gpsimd", lambda en: en.memset(carry[:], 0.0),
          reads=(), writes=tuple(f"carry_{k}" for k in range(256)))
        for _aj in range(2):
            E("gpsimd", (lambda aj=_aj: lambda en: en.memset(apair[aj][:], 0.0))(),
              reads=(), writes=(f"apair{_aj}_0", f"apair{_aj}_1"))
        _cvals = [LN_EPS, -0.5, -1.0, 1.0, INV127, INV127SQ, 1.0 / D]
        for _ci, _cv in enumerate(_cvals):
            E("gpsimd", (lambda ci_=_ci, cv_=_cv: lambda en: en.memset(
                constp[:, ci_:ci_ + 1], cv_))(), reads=(), writes=("constp",))
        E("qpool", lambda en: en.dma_start(out=bwb[:], in_=bw_d),
          reads=(), writes=("bwb",), dkey="qbwb")
        E("qpool", lambda en: en.dma_start(out=cwb[:], in_=cw_d),
          reads=(), writes=("cwb",), dkey="qcwb")
        E("qsp", lambda en: en.dma_start(out=sp_a[:, 0:256], in_=logA_d),
          reads=(), writes=(T_SPA,), dkey="qsp_a")
        E("scalar", lambda en: en.activation(out=sp_r[:, 0:256], in_=sp_a[:, 0:256], func=AF.Exp),
          reads=(T_SPA,), writes=(T_SPR,))
        E("vector", lambda en: en.tensor_scalar_mul(out=acol[:], in0=sp_r[:, 0:256], scalar1=constp[:, 2:3]),
          reads=(T_SPR, "constp"), writes=("acol",))
        E("qsp", lambda en: en.dma_start(out=bbs[:], in_=bb_d), reads=(), writes=("bbs",), dkey="qbbs")
        E("qsp", lambda en: en.dma_start(out=cbs[:], in_=cb_d), reads=(), writes=("cbs",), dkey="qcbs")
        E("qsp", lambda en: en.dma_start(out=dtbs[:], in_=dtb_d), reads=(), writes=("dtbs",), dkey="qdtbs")
        E("qsp", lambda en: en.dma_start(out=dsks[:], in_=dsk_d), reads=(), writes=("dsks",), dkey="qdsks")
        E("qsp", lambda en: en.dma_start(out=identb[:], in_=identb_d), reads=(), writes=("identb",), dkey="qident")

        def emitA(tc):
            v = tc % 2
            t0 = tc * TC
            for i in range(2):
                E("qsp", (lambda i_=i, t0_=t0: lambda en: en.dma_start(
                    out=x_tm[i_][:], in_=x_d[t0_ + i_ * 128: t0_ + (i_ + 1) * 128, :]))(),
                  reads=(), writes=(f"x_tm{i}",), dkey=f"qx{i}")
            for i in range(NSL):
                xi = x_tm[i % 2]
                xn_i = xn_tm[i]
                E("vector", (lambda xi_=xi: lambda en: en.reduce_sum(
                    out=sx[:], in_=xi_[:], axis=AX.X))(),
                  reads=(f"x_tm{i % 2}",), writes=("sx",))
                E("scalar", (lambda xi_=xi, xn_=xn_i: lambda en: en.activation(
                    out=xn_[:], in_=xi_[:], func=AF.Square, accum_out=sx2[:]))(),
                  reads=(f"x_tm{i % 2}",), writes=(f"xn_tm{i}", "sx2"))
                E("vector", lambda en: en.tensor_scalar_mul(out=mvec[:], in0=sx[:], scalar1=constp[:, 6:7]),
                  reads=("sx", "constp"), writes=("mvec",))
                E("vector", lambda en: en.tensor_mul(msq[:], mvec[:], mvec[:]),
                  reads=(), sync_reads=("mvec",), writes=("msq",))
                E("vector", lambda en: en.tensor_scalar(
                    out=var[:], in0=sx2[:], scalar1=constp[:, 6:7], scalar2=msq[:],
                    op0=AL.mult, op1=AL.subtract),
                  reads=("sx2", "constp"), sync_reads=("msq",), writes=("var",))
                E("scalar", lambda en: en.activation(out=lnv[:], in_=var[:], func=AF.Ln, bias=constp[:, 0:1]),
                  reads=("var", "constp"), writes=("lnv",))
                E("scalar", lambda en: en.activation(out=rstd[:], in_=lnv[:], func=AF.Exp, scale=constp[:, 1:2]),
                  reads=("lnv", "constp"), writes=("rstd",))
                E("vector", lambda en: en.tensor_scalar(
                    out=negmr[:], in0=mvec[:], scalar1=rstd[:], scalar2=constp[:, 2:3],
                    op0=AL.mult, op1=AL.mult),
                  reads=("rstd", "constp"), sync_reads=("mvec",), writes=("negmr",))
                E("scalar", (lambda xi_=xi, xn_=xn_i: lambda en: en.activation(
                    out=xn_[:], in_=xi_[:], func=AF.Identity, scale=rstd[:], bias=negmr[:]))(),
                  reads=(f"x_tm{i % 2}", "negmr"), sync_reads=("rstd",), writes=(f"xn_tm{i}",))
                if i + 2 < NSL:
                    E("qsp", (lambda i_=i, t0_=t0: lambda en: en.dma_start(
                        out=x_tm[(i_ + 2) % 2][:],
                        in_=x_d[t0_ + (i_ + 2) * 128: t0_ + (i_ + 3) * 128, :]))(),
                      reads=(), writes=(f"x_tm{(i + 2) % 2}",), dkey=f"qx{(i + 2) % 2}")
            for c in range(KC):
                for i in range(NSL):
                    E("tensor", (lambda c_=c, i_=i: lambda en: en.transpose(
                        out=PSBF[:, i_ * 128:(i_ + 1) * 128],
                        in_=xn_tm[i_][:, c_ * 128:(c_ + 1) * 128], identity=identb[:]))(),
                      reads=(f"xn_tm{i}", "identb"), writes=("psbf",))
                E("scalar", (lambda c_=c, v_=v: lambda en: en.copy(out=xn_fm[v_][c_][:], in_=PSBF[:]))(),
                  reads=("psbf",), writes=(f"xn_fm{v}_{c}",))

        def emitB(tc):
            v = tc % 2
            E("qpool", lambda en: en.dma_start(out=dtw_buf[0][:], in_=dtw_d[:, 0:1024]),
              reads=(), writes=("dtw_buf0",), dkey="qdtw0")
            for ec in range(KC):
                j = ec % 2
                if ec + 1 < KC:
                    jn = (ec + 1) % 2
                    E("qpool", (lambda ec_=ec, jn_=jn: lambda en: en.dma_start(
                        out=dtw_buf[jn_][:], in_=dtw_d[:, (ec_ + 1) * 1024:(ec_ + 2) * 1024]))(),
                      reads=(), writes=(f"dtw_buf{jn}",), dkey=f"qdtw{jn}")
                pdst, ptok = (PS0, "ps0") if ec % 2 == 0 else (PS3, "ps3")
                for kc in range(KC):
                    E("tensor", (lambda j_=j, kc_=kc, pd_=pdst: lambda en: en.matmul(
                        out=pd_[:], lhsT=dtw_buf[j_][:, kc_ * 128:(kc_ + 1) * 128],
                        rhs=xn_fm[v][kc_][:], start=(kc_ == 0), stop=(kc_ == KC - 1)))(),
                      reads=(f"dtw_buf{j}", f"xn_fm{v}_{kc}", ptok), writes=(ptok,))
                E("scalar", (lambda ec_=ec, pd_=pdst: lambda en: en.activation(
                    out=sp_r[:], in_=pd_[:], func=AF.Relu, bias=dtbs[:, ec_:ec_ + 1]))(),
                  reads=(ptok, "dtbs"), writes=(T_SPR,))
                E("scalar", (lambda ec_=ec, pd_=pdst: lambda en: en.activation(
                    out=sp_a[:], in_=pd_[:], func=AF.Abs, bias=dtbs[:, ec_:ec_ + 1]))(),
                  reads=(ptok, "dtbs"), writes=(T_SPA,))
                E("scalar", lambda en: en.activation(out=sp_e[:], in_=sp_a[:], func=AF.Exp, scale=constp[:, 2:3]),
                  reads=(T_SPA, "constp"), writes=(T_SPE,))
                E("scalar", lambda en: en.activation(out=sp_a[:], in_=sp_e[:], func=AF.Ln, bias=constp[:, 3:4]),
                  reads=(T_SPE, "constp"), writes=(T_SPA,))
                E("vector", (lambda ec_=ec, v_=v: lambda en: en.tensor_add(
                    dt_fm[v_][ec_][:], sp_r[:], sp_a[:]))(),
                  reads=(T_SPR, T_SPA), writes=(f"dt_fm{v}_{ec}",))
                E("vector", (lambda ec_=ec, v_=v: lambda en: en.tensor_mul(
                    ub[v_][ec_][:], dt_fm[v_][ec_][:], xn_fm[v_][ec_][:]))(),
                  reads=(f"dt_fm{v}_{ec}", f"xn_fm{v}_{ec}"), writes=(f"ub{v}_{ec}",))
            for kc in range(KC):
                E("tensor", (lambda kc_=kc, v_=v: lambda en: en.matmul(
                    out=PSB[:], lhsT=bwb[:, kc_ * 32:(kc_ + 1) * 32], rhs=xn_fm[v_][kc_][:],
                    start=(kc_ == 0), stop=(kc_ == KC - 1)))(),
                  reads=("bwb", f"xn_fm{v}_{kc}", "psb"), writes=("psb",))
            E("scalar", lambda en: en.activation(out=b_sb[:], in_=PSB[:], func=AF.Identity, bias=bbs[:]),
              reads=("psb", "bbs"), writes=("b_sb",))
            for kc in range(KC):
                E("tensor", (lambda kc_=kc, v_=v: lambda en: en.matmul(
                    out=PSC[:], lhsT=cwb[:, kc_ * 32:(kc_ + 1) * 32], rhs=xn_fm[v_][kc_][:],
                    start=(kc_ == 0), stop=(kc_ == KC - 1)))(),
                  reads=("cwb", f"xn_fm{v}_{kc}", "psc"), writes=("psc",))
            E("scalar", lambda en: en.activation(out=c_sb[:], in_=PSC[:], func=AF.Identity, bias=cbs[:]),
              reads=("psc", "cbs"), writes=("c_sb",))
            E("qpool", (lambda v_=v: lambda en: en.dma_start(out=bscr[v_], in_=b_sb[:]))(),
              reads=("b_sb",), writes=(f"bscr{v}",), dkey="qbscr")
            E("qpool", (lambda v_=v: lambda en: en.dma_start(out=cscr[v_], in_=c_sb[:]))(),
              reads=("c_sb",), writes=(f"cscr{v}",), dkey="qcscr")

        def emitByinit(tc):
            v = tc % 2
            for ec in range(KC):
                E("vector", (lambda ec_=ec, v_=v: lambda en: en.tensor_scalar_mul(
                    out=y_acc[ec_][:], in0=xn_fm[v_][ec_][:], scalar1=dsks[:, ec_:ec_ + 1]))(),
                  reads=(f"xn_fm{v}_{ec}", "dsks"), writes=(f"y_acc{ec}",))

        def emitC_deferred(tc):
            sch.defer = []
            emitC(tc)
            out = sch.defer
            sch.defer = None
            return out

        def emitC(tc):
            v = tc % 2

            def load_gu(hc_):
                j = hc_ % 2
                E("qpool", (lambda h_=hc_, j_=j: lambda en: en.dma_start(
                    out=wgbuf[j_][:], in_=wg8_d[h_ * 128:(h_ + 1) * 128, :]))(),
                  reads=(), writes=(f"wgbuf{j}",), dkey=f"qwg{j}")
                E("qpool", (lambda h_=hc_, j_=j: lambda en: en.dma_start(
                    out=wubuf[j_][:], in_=wu8_d[h_ * 128:(h_ + 1) * 128, :]))(),
                  reads=(), writes=(f"wubuf{j}",), dkey=f"qwu{j}")

            load_gu(0)
            for hcp_ in range(HC // 2):
                for sub in range(2):
                    hc = 2 * hcp_ + sub
                    if hc + 1 < HC:
                        load_gu(hc + 1)
                    gps, gtok = (PS1, "ps1") if sub == 0 else (PS0, "ps0")
                    ups, utok = (PS2, "ps2") if sub == 0 else (PS3, "ps3")
                    j = hc % 2
                    for kc in range(KC):
                        E("tensor", (lambda j_=j, kc_=kc, v_=v, g_=gps: lambda en: en.matmul(
                            out=g_[:], lhsT=wgbuf[j_][:, kc_ * 128:(kc_ + 1) * 128],
                            rhs=xn_fm[v_][kc_][:], start=(kc_ == 0), stop=(kc_ == KC - 1)))(),
                          reads=(f"wgbuf{j}", f"xn_fm{v}_{kc}", gtok), writes=(gtok,))
                    for kc in range(KC):
                        E("tensor", (lambda j_=j, kc_=kc, v_=v, u_=ups: lambda en: en.matmul(
                            out=u_[:], lhsT=wubuf[j_][:, kc_ * 128:(kc_ + 1) * 128],
                            rhs=xn_fm[v_][kc_][:], start=(kc_ == 0), stop=(kc_ == KC - 1)))(),
                          reads=(f"wubuf{j}", f"xn_fm{v}_{kc}", utok), writes=(utok,))
                for sub in range(2):
                    hc = 2 * hcp_ + sub
                    gps, gtok = (PS1, "ps1") if sub == 0 else (PS0, "ps0")
                    E("scalar", (lambda s_=sub, g_=gps: lambda en: en.activation(
                        out=sg_b[s_][:], in_=g_[:], func=AF.Silu, scale=constp[:, 4:5]))(),
                      reads=(gtok, "constp"), writes=(f"sg_b{sub}",))
                for sub in range(2):
                    hc = 2 * hcp_ + sub
                    ups, utok = (PS2, "ps2") if sub == 0 else (PS3, "ps3")
                    E("scalar", (lambda s_=sub, u_=ups: lambda en: en.copy(
                        out=up_b[s_][:], in_=u_[:]))(),
                      reads=(utok,), writes=(f"up_b{sub}",))
                for sub in range(2):
                    hc = 2 * hcp_ + sub
                    E("gpsimd", (lambda s_=sub, h_=hc: lambda en: en.tensor_mul(
                        h_act[h_][:], sg_b[s_][:], up_b[s_][:]))(),
                      reads=(f"sg_b{sub}", f"up_b{sub}"), writes=(f"h_act{hc}",))
            for c in range(KC):
                pdst, ptok = (PS1, "ps1") if c % 2 == 0 else (PS2, "ps2")
                for half in range(2):
                    j = (2 * c + half) % 2
                    E("qpool", (lambda c_=c, hf_=half, j_=j: lambda en: en.dma_start(
                        out=wdhalf[j_][:],
                        in_=wd8_d[c_ * 128:(c_ + 1) * 128, hf_ * 1536:(hf_ + 1) * 1536]))(),
                      reads=(), writes=(f"wdhalf{j}",), dkey=f"qwd{j}")
                for half in range(2):
                    j = (2 * c + half) % 2
                    for h12 in range(12):
                        h = half * 12 + h12
                        E("tensor", (lambda j_=j, h12_=h12, h_=h, pd_=pdst: lambda en: en.matmul(
                            out=pd_[:], lhsT=wdhalf[j_][:, h12_ * 128:(h12_ + 1) * 128],
                            rhs=h_act[h_][:], start=(h_ == 0), stop=(h_ == HC - 1)))(),
                          reads=(f"wdhalf{j}", f"h_act{h}", ptok), writes=(ptok,))
                E("scalar", (lambda c_=c, pd_=pdst, v_=v: lambda en: en.activation(
                    out=g_fm[v_][c_][:], in_=pd_[:], func=AF.Silu, scale=constp[:, 5:6]))(),
                  reads=(ptok, "constp"), writes=(f"g_fm{v}_{c}",))

        def emitD(tc, cdefer=()):
            cq = list(cdefer)
            ci = 0

            def pump(k):
                nonlocal ci
                for _ in range(k):
                    if ci < len(cq):
                        e_, f_, r_, w_, s_, d_ = cq[ci]
                        ci += 1
                        sch.emit(e_, f_, r_, w_, s_, d_)

            v = tc % 2
            t0e = tc * TC
            for i in range(2):
                E("qsp", (lambda i_=i, t0_=t0e: lambda en: en.dma_start(
                    out=x_re[i_][:], in_=x_d[t0_ + i_ * 128: t0_ + (i_ + 1) * 128, :]))(),
                  reads=(), writes=(f"x_re{i}",), dkey=f"qxr{i}")

            def bcast(np2):
                j2 = np2 % 2
                for half in range(2):
                    n_ = 2 * np2 + half
                    E("qsp", (lambda n2=n_, j_=j2, h_=half, v_=v: lambda en: en.dma_start(
                        out=btp[j_][:, h_, :], in_=bscr[v_][n2:n2 + 1, :].partition_broadcast(128)))(),
                      reads=(f"bscr{v}",), writes=(f"btp{j2}_{half}",), dkey=f"qbt{2 * j2 + half}")
                    E("qsp", (lambda n2=n_, j_=j2, h_=half, v_=v: lambda en: en.dma_start(
                        out=ctp[j_][:, h_, :], in_=cscr[v_][n2:n2 + 1, :].partition_broadcast(128)))(),
                      reads=(f"cscr{v}",), writes=(f"ctp{j2}_{half}",), dkey=f"qct{2 * j2 + half}")

            bcast(0)
            kctr = 0
            pairctr = 0
            for np_ in range(NST // 2):
                n0, n1 = 2 * np_, 2 * np_ + 1
                j2 = np_ % 2
                if np_ + 1 < NST // 2:
                    bcast(np_ + 1)
                for c in range(KC):
                    kp = pairctr % 3
                    hj = pairctr % 2
                    ap_ = pairctr % 2
                    pairctr += 1
                    for (n_, half) in ((n0, 0), (n1, 1)):
                        E("scalar", (lambda c_=c, n2=n_, a_=ap_, h_=half, v_=v: lambda en: en.activation(
                            out=apair[a_][:, h_, 1:TC + 1], in_=dt_fm[v_][c_][:], func=AF.Exp,
                            scale=acol[:, c_ * 32 + n2: c_ * 32 + n2 + 1]))(),
                          reads=(f"dt_fm{v}_{c}", "acol"), writes=(f"apair{ap_}_{half}",))
                    for (n_, half) in ((n0, 0), (n1, 1)):
                        slot = c * 32 + n_
                        E("scalar", (lambda k_=kp, h_=half, sl_=slot: lambda en: en.copy(
                            out=wpair[k_][:, h_, 0:1], in_=carry[:, sl_:sl_ + 1]))(),
                          reads=(f"carry_{slot}",), writes=(f"wpair{kp}_inj",))
                    E("vector", (lambda c_=c, j_=j2, k_=kp, v_=v: lambda en: en.tensor_mul(
                        wpair[k_][:, :, 1:TC + 1],
                        ub[v_][c_][:].rearrange("p (o t) -> p o t", o=1).broadcast_to([128, 2, TC]),
                        btp[j_][:]))(),
                      reads=(f"ub{v}_{c}", f"btp{j2}_0", f"btp{j2}_1"), writes=(f"wpair{kp}",))
                    E("vector", (lambda a_=ap_, kp_=kp: lambda en: en.tensor_tensor_scan(
                        out=hpair[kp_][:].rearrange("p a t -> p (a t)"),
                        data0=apair[a_][:].rearrange("p a t -> p (a t)"),
                        data1=wpair[kp_][:].rearrange("p a t -> p (a t)"),
                        initial=0.0, op0=AL.mult, op1=AL.add))(),
                      reads=(f"apair{ap_}_0", f"apair{ap_}_1", f"wpair{kp}", f"wpair{kp}_inj"),
                      writes=(f"hpair{kp}_0", f"hpair{kp}_1"))
                    if tc + 1 < NTC:
                        for (n_, half) in ((n0, 0), (n1, 1)):
                            slot = c * 32 + n_
                            E("gpsimd", (lambda kp_=kp, h_=half, sl_=slot: lambda en: en.tensor_copy(
                                out=carry[:, sl_:sl_ + 1], in_=hpair[kp_][:, h_, TC:TC + 1]))(),
                              reads=(f"hpair{kp}_{half}",), writes=(f"carry_{slot}",))
                    E("vector", (lambda j_=j2, kp_=kp, hj_=hj: lambda en: en.tensor_mul(
                        hcp[hj_][:], hpair[kp_][:, :, 1:TC + 1], ctp[j_][:]))(),
                      reads=(f"hpair{kp}_0", f"hpair{kp}_1", f"ctp{j2}_0", f"ctp{j2}_1"),
                      writes=(f"hcp{hj}",))
                    pump(6)
                    if np_ % 2 == 0:
                        E("vector", (lambda c_=c, hj_=hj: lambda en: en.tensor_add(
                            pairc[c_][:], hcp[hj_][:, 0, :], hcp[hj_][:, 1, :]))(),
                          reads=(f"hcp{hj}",), writes=(f"pairc{c}",))
                    else:
                        E("vector", (lambda hj_=hj: lambda en: en.tensor_add(
                            quadb[0][:], hcp[hj_][:, 0, :], hcp[hj_][:, 1, :]))(),
                          reads=(f"hcp{hj}",), writes=("quadb0",))
                        E("vector", (lambda c_=c: lambda en: en.tensor_add(
                            quadb[0][:], pairc[c_][:], quadb[0][:]))(),
                          reads=(f"pairc{c}", "quadb0"), writes=("quadb0",))
                        E("vector", (lambda c_=c: lambda en: en.tensor_add(
                            y_acc[c_][:], y_acc[c_][:], quadb[0][:]))(),
                          reads=(f"y_acc{c}", "quadb0"), writes=(f"y_acc{c}",))
            pump(10 ** 9)

        def emitE(tc):
            v = tc % 2
            t0 = tc * TC
            for c in range(KC):
                E("vector", (lambda c_=c, v_=v: lambda en: en.tensor_mul(
                    ub[v_][c_][:], y_acc[c_][:], g_fm[v_][c_][:]))(),
                  reads=(f"y_acc{c}", f"g_fm{v}_{c}"), writes=(f"ub{v}_{c}",))
            for i in range(NSL):
                jo = 0
                for ch in range(2):
                    pb, pbtok = (PSBF, "psbf") if (2 * i + ch) % 2 == 0 else (PSBF2, "psbf2")
                    for c4 in range(4):
                        c = ch * 4 + c4
                        E("tensor", (lambda c_=c, i_=i, c4_=c4, pb_=pb, v_=v: lambda en: en.transpose(
                            out=pb_[:, c4_ * 128:(c4_ + 1) * 128],
                            in_=ub[v_][c_][:, i_ * 128:(i_ + 1) * 128], identity=identb[:]))(),
                          reads=(f"ub{v}_{c}", "identb"), writes=(pbtok,))
                    E("vector", (lambda i_=i, ch_=ch, jo_=jo, pb_=pb: lambda en: en.tensor_add(
                        out_tm[jo_][:, ch_ * 512:(ch_ + 1) * 512], pb_[:],
                        x_re[i_ % 2][:, ch_ * 512:(ch_ + 1) * 512]))(),
                      reads=(pbtok, f"x_re{i % 2}"), writes=(f"out_tm{jo}",))
                E("qsp", (lambda i_=i, jo_=jo, t0_=t0: lambda en: en.dma_start(
                    out=out_d[t0_ + i_ * 128: t0_ + (i_ + 1) * 128, :], in_=out_tm[jo_][:]))(),
                  reads=(f"out_tm{jo}",), writes=(f"outdma{tc}_{i}",), dkey=f"qout{jo}")
                if i + 2 < NSL:
                    E("qsp", (lambda i_=i, t0_=t0: lambda en: en.dma_start(
                        out=x_re[(i_ + 2) % 2][:],
                        in_=x_d[t0_ + (i_ + 2) * 128: t0_ + (i_ + 3) * 128, :]))(),
                      reads=(), writes=(f"x_re{(i + 2) % 2}",), dkey=f"qxr{(i + 2) % 2}")

        emitA(0)
        emitB(0)
        emitByinit(0)
        for tc in range(NTC):
            if tc + 1 < NTC:
                emitA(tc + 1)
                emitB(tc + 1)
            emitD(tc, emitC_deferred(tc))
            emitE(tc)
            if tc + 1 < NTC:
                emitByinit(tc + 1)

        for (eng_, fn_, w_, skey_, inc_) in sch.recs:
            assert not (eng_ in ("qsp", "qpool", "qact") and skey_ == eng_), "DMA record missing dkey"
        for sk in sch.scount:
            sems[sk] = es.enter_context(nc.semaphore(("s_" + str(sk))[:30]))

        def build_body(body_engs):
            def run(eng_obj):
                for (eng, fn, waits, skey, inc) in sch.recs:
                    if eng not in body_engs:
                        continue
                    ws = sorted(waits.items(), key=lambda kv: str(kv[0]))
                    for de, dv in ws[:-1]:
                        eng_obj.wait_ge(sems[de], dv)
                    ins = fn(eng_obj)
                    if ws:
                        ins = ins._wait_ge(sems[ws[-1][0]], ws[-1][1])
                    ins.then_inc(sems[skey], inc)
            return run

        block.tensor(build_body(("tensor",)))
        block.vector(build_body(("vector",)))
        block.scalar(build_body(("scalar", "qact")))
        block.gpsimd(build_body(("gpsimd", "qpool")))
        out_keys = [k for k in sch.scount if str(k).startswith("qout")]

        def sync_body(eng_obj):
            build_body(("qsp",))(eng_obj)
            for k in out_keys:
                eng_obj.wait_ge(sems[k], sch.scount[k])
        block.sync(sync_body)

    return nc


_CACHE = {}


def _get_nc(debug=False):
    key = ("nc", debug)
    if key not in _CACHE:
        _CACHE[key] = build_program()
    return _CACHE[key]


def kernel(L0_attn_x, L0_attn_log_A, L0_attn_B_w, L0_attn_B_b, L0_attn_C_w,
           L0_attn_C_b, L0_attn_D_skip, L0_attn_dt_w, L0_attn_dt_b,
           L0_ffn_w_gate, L0_ffn_w_up, L0_ffn_w_down):
    nc = _get_nc()
    f32 = np.float32
    x = np.ascontiguousarray(np.asarray(L0_attn_x, dtype=f32))
    dt_w = np.ascontiguousarray(np.asarray(L0_attn_dt_w, dtype=f32))
    B_w = np.ascontiguousarray(np.asarray(L0_attn_B_w, dtype=f32))
    C_w = np.ascontiguousarray(np.asarray(L0_attn_C_w, dtype=f32))
    log_A = np.ascontiguousarray(np.asarray(L0_attn_log_A, dtype=f32))

    dtw = np.ascontiguousarray(
        dt_w.reshape(8, 128, 8, 128).transpose(3, 0, 2, 1)).reshape(128, 8192)
    bw = np.ascontiguousarray(B_w.reshape(32, 8, 128).transpose(2, 1, 0)).reshape(128, 256)
    cw = np.ascontiguousarray(C_w.reshape(32, 8, 128).transpose(2, 1, 0)).reshape(128, 256)
    logA = np.ascontiguousarray(log_A.reshape(8, 128, 32).transpose(1, 0, 2)).reshape(128, 256)
    bb = np.ascontiguousarray(np.asarray(L0_attn_B_b, dtype=f32).reshape(32, 1))
    cb = np.ascontiguousarray(np.asarray(L0_attn_C_b, dtype=f32).reshape(32, 1))
    dtb = np.ascontiguousarray(np.asarray(L0_attn_dt_b, dtype=f32).reshape(8, 128).T)
    dsk = np.ascontiguousarray(np.asarray(L0_attn_D_skip, dtype=f32).reshape(8, 128).T)

    def int8_view(w):
        w = np.ascontiguousarray(np.asarray(w, dtype=np.int32))
        return w.view(np.int8).reshape(w.shape[0], w.shape[1], 4)[:, :, 0]

    wg8 = np.ascontiguousarray(
        int8_view(L0_ffn_w_gate).reshape(24, 128, 8, 128).transpose(0, 3, 2, 1)).reshape(HID, D)
    wu8 = np.ascontiguousarray(
        int8_view(L0_ffn_w_up).reshape(24, 128, 8, 128).transpose(0, 3, 2, 1)).reshape(HID, D)
    wd8 = np.ascontiguousarray(
        int8_view(L0_ffn_w_down).reshape(8, 128, 24, 128).transpose(0, 3, 2, 1)).reshape(D, HID)
    identb = np.eye(128, dtype=ml_dtypes.bfloat16)

    common = dict(dtw=dtw, bw=bw, cw=cw, bb=bb, cb=cb, dtb=dtb, dsk=dsk,
                  logA=logA, wg8=wg8, wu8=wu8, wd8=wd8, identb_in=identb)
    in_maps = [dict(common, x=np.ascontiguousarray(x[k])) for k in range(8)]
    res = run_bass_kernel_spmd(nc, in_maps, core_ids=list(range(8)))
    out = np.stack([np.asarray(r["out"], dtype=f32) for r in res.results])
    return out


# version 31
# speedup vs baseline: 1.1327x; 1.0274x over previous
import contextlib
import numpy as np
import ml_dtypes
import concourse.bass as bass
import concourse.mybir as mybir
from concourse.bass_utils import run_bass_kernel_spmd

F32 = mybir.dt.float32
BF16 = mybir.dt.bfloat16
I8 = mybir.dt.int8
AL = mybir.AluOpType
AF = mybir.ActivationFunctionType
AX = mybir.AxisListType

S, D, NST, HID = 2048, 1024, 32, 3072
TC = 512
NTC = S // TC
KC = D // 128
HC = HID // 128
NSL = TC // 128
INV127 = float(1.0 / 127.0)
INV127SQ = float(1.0 / (127.0 * 127.0))
LN_EPS = 1e-5

COMPUTE = ("tensor", "vector", "scalar", "gpsimd")
BODY_OF = {"tensor": "tensor", "vector": "vector", "scalar": "scalar",
           "gpsimd": "gpsimd", "qsp": "sync", "qpool": "gpsimd", "qact": "scalar"}


class Sched:

    def __init__(self):
        self.recs = []
        self.scount = {}
        self.last_w = {}
        self.readers = {}
        self.known = {}
        self.unwritten = set()
        self.defer = None

    def emit(self, eng, fn, reads=(), writes=(), sync_reads=(), dkey=None):
        if self.defer is not None:
            self.defer.append((eng, fn, reads, writes, sync_reads, dkey))
            return
        body = BODY_OF[eng]
        skey = dkey if dkey is not None else eng
        inc = 16 if dkey is not None else 1
        myval = self.scount.get(skey, 0) + inc
        self.scount[skey] = myval
        kn = self.known.setdefault(body, {})
        waits = {}

        def need(dep, force=False):
            dk, dv = dep
            if dk == eng and dkey is None and not force:
                return
            if kn.get(dk, 0) >= dv:
                return
            if waits.get(dk, 0) < dv:
                waits[dk] = dv

        for t in sync_reads:
            if t in self.last_w:
                need(self.last_w[t], force=True)
            else:
                self.unwritten.add(t)
        for t in reads:
            if t in self.last_w:
                need(self.last_w[t])
            else:
                self.unwritten.add(t)
        for t in writes:
            if t in self.last_w:
                need(self.last_w[t])
            for dk, dv in self.readers.get(t, {}).items():
                need((dk, dv))
        for dk, dv in waits.items():
            kn[dk] = dv
        for t in writes:
            self.last_w[t] = (skey, myval)
            self.readers[t] = {}
        for t in list(reads) + list(sync_reads):
            rd = self.readers.setdefault(t, {})
            if rd.get(skey, 0) < myval:
                rd[skey] = myval
        self.recs.append((eng, fn, waits, skey, inc))


def build_program():
    nc = bass.Bass("TRN2", target_bir_lowering=False)

    x_d = nc.dram_tensor("x", [S, D], F32, kind="ExternalInput").ap()
    dtw_d = nc.dram_tensor("dtw", [128, 8192], F32, kind="ExternalInput").ap()
    bw_d = nc.dram_tensor("bw", [128, 256], F32, kind="ExternalInput").ap()
    cw_d = nc.dram_tensor("cw", [128, 256], F32, kind="ExternalInput").ap()
    bb_d = nc.dram_tensor("bb", [32, 1], F32, kind="ExternalInput").ap()
    cb_d = nc.dram_tensor("cb", [32, 1], F32, kind="ExternalInput").ap()
    dtb_d = nc.dram_tensor("dtb", [128, 8], F32, kind="ExternalInput").ap()
    dsk_d = nc.dram_tensor("dsk", [128, 8], F32, kind="ExternalInput").ap()
    logA_d = nc.dram_tensor("logA", [128, 256], F32, kind="ExternalInput").ap()
    wg8_d = nc.dram_tensor("wg8", [HID, D], I8, kind="ExternalInput").ap()
    wu8_d = nc.dram_tensor("wu8", [HID, D], I8, kind="ExternalInput").ap()
    wd8_d = nc.dram_tensor("wd8", [D, HID], I8, kind="ExternalInput").ap()
    identb_d = nc.dram_tensor("identb_in", [128, 128], BF16, kind="ExternalInput").ap()
    out_d = nc.dram_tensor("out", [S, D], F32, kind="ExternalOutput").ap()
    bscr = [nc.dram_tensor(f"bscr{v}", [32, TC], BF16, kind="Internal").ap() for v in range(2)]
    cscr = [nc.dram_tensor(f"cscr{v}", [32, TC], BF16, kind="Internal").ap() for v in range(2)]

    es = contextlib.ExitStack()
    with es:
        def sb(name, shape, dtype):
            return es.enter_context(nc.sbuf_tensor(name, shape, dtype))

        def ps(name, shape, dtype):
            return es.enter_context(nc.psum_tensor(name, shape, dtype))

        bwb = sb("bwb", [128, 256], BF16)
        cwb = sb("cwb", [128, 256], BF16)
        bbs = sb("bbs", [32, 1], F32)
        cbs = sb("cbs", [32, 1], F32)
        dtbs = sb("dtbs", [128, 8], F32)
        dsks = sb("dsks", [128, 8], F32)
        acol = sb("acol", [128, 256], F32)
        identb = sb("identb", [128, 128], BF16)
        constp = sb("constp", [128, 8], F32)
        sx = sb("sx", [128, 1], F32)
        sx2 = sb("sx2", [128, 1], F32)
        mvec = sb("mvec", [128, 1], F32)
        msq = sb("msq", [128, 1], F32)
        var = sb("var", [128, 1], F32)
        lnv = sb("lnv", [128, 1], F32)
        rstd = sb("rstd", [128, 1], F32)
        negmr = sb("negmr", [128, 1], F32)
        x_tm = [sb(f"x_tm{i}", [128, 1024], F32) for i in range(2)]
        x_re = [sb(f"x_re{i}", [128, 1024], F32) for i in range(2)]
        xn_tm = [sb(f"xn_tm{i}", [128, 1024], BF16) for i in range(NSL)]
        xn_fm = [[sb(f"xn_fm{v}_{c}", [128, TC], BF16) for c in range(KC)] for v in range(2)]
        dt_fm = [[sb(f"dt_fm{v}_{c}", [128, TC], BF16) for c in range(KC)] for v in range(2)]
        ub = [[sb(f"ub{v}_{c}", [128, TC], BF16) for c in range(KC)] for v in range(2)]
        sp_r = sb("sp_r", [128, TC], BF16)
        sp_a = sb("sp_a", [128, TC], F32)
        sp_e = sb("sp_e", [128, TC], BF16)
        b_sb = sb("b_sb", [32, TC], BF16)
        c_sb = sb("c_sb", [32, TC], BF16)
        dtw_buf = [sb(f"dtw_buf{j}", [128, 1024], BF16) for j in range(2)]
        wgbuf = [sb(f"wgbuf{j}", [128, 1024], BF16) for j in range(2)]
        wubuf = [sb(f"wubuf{j}", [128, 1024], BF16) for j in range(2)]
        wdhalf = [sb(f"wdhalf{j}", [128, 1536], BF16) for j in range(2)]
        sg_b = [sb(f"sg_b{j}", [128, TC], BF16) for j in range(2)]
        up_b = [sb(f"up_b{j}", [128, TC], BF16) for j in range(2)]
        h_act = [sb(f"h_act{h}", [128, TC], BF16) for h in range(HC)]
        g_fm = [[sb(f"g_fm{v}_{c}", [128, TC], BF16) for c in range(KC)] for v in range(2)]
        btp = [sb(f"btp{j}", [128, 2, TC], BF16) for j in range(2)]
        ctp = [sb(f"ctp{j}", [128, 2, TC], BF16) for j in range(2)]
        apair = [sb(f"apair{j}", [128, 2, TC + 1], F32) for j in range(2)]
        wpair = [sb(f"wpair{j}", [128, 2, TC + 1], BF16) for j in range(3)]
        hpair = [sb(f"hpair{j}", [128, 2, TC + 1], BF16) for j in range(3)]
        hcp = [sb(f"hcp{j}", [128, 2, TC], BF16) for j in range(2)]
        pairc = [sb(f"pairc{c}", [128, TC], BF16) for c in range(KC)]
        quadb = [sb(f"quadb{j}", [128, TC], BF16) for j in range(1)]
        y_acc = [sb(f"y_acc{c}", [128, TC], BF16) for c in range(KC)]
        carry = sb("carry", [128, 256], F32)
        out_tm = [sb(f"out_tm{j}", [128, 1024], F32) for j in range(1)]

        PS0 = ps("PS0", [128, TC], F32)
        PS1 = ps("PS1", [128, TC], F32)
        PS2 = ps("PS2", [128, TC], F32)
        PS3 = ps("PS3", [128, TC], F32)
        PSB = ps("PSB", [32, TC], F32)
        PSC = ps("PSC", [32, TC], F32)
        PSBF = ps("PSBF", [128, TC], BF16)
        PSBF2 = ps("PSBF2", [128, TC], BF16)

        sems = {}
        block = es.enter_context(nc.Block())

        sch = Sched()
        E = sch.emit
        T_SPR = ("stage", "sp_r")
        T_SPA = ("stage", "sp_a")
        T_SPE = ("stage", "sp_e")

        E("gpsimd", lambda en: en.memset(carry[:], 0.0),
          reads=(), writes=tuple(f"carry_{k}" for k in range(256)))
        for _aj in range(2):
            E("gpsimd", (lambda aj=_aj: lambda en: en.memset(apair[aj][:], 0.0))(),
              reads=(), writes=(f"apair{_aj}_0", f"apair{_aj}_1"))
        _cvals = [LN_EPS, -0.5, -1.0, 1.0, INV127, INV127SQ, 1.0 / D]
        for _ci, _cv in enumerate(_cvals):
            E("gpsimd", (lambda ci_=_ci, cv_=_cv: lambda en: en.memset(
                constp[:, ci_:ci_ + 1], cv_))(), reads=(), writes=("constp",))
        E("qpool", lambda en: en.dma_start(out=bwb[:], in_=bw_d),
          reads=(), writes=("bwb",), dkey="qbwb")
        E("qpool", lambda en: en.dma_start(out=cwb[:], in_=cw_d),
          reads=(), writes=("cwb",), dkey="qcwb")
        E("qsp", lambda en: en.dma_start(out=sp_a[:, 0:256], in_=logA_d),
          reads=(), writes=(T_SPA,), dkey="qsp_a")
        E("scalar", lambda en: en.activation(out=sp_r[:, 0:256], in_=sp_a[:, 0:256], func=AF.Exp),
          reads=(T_SPA,), writes=(T_SPR,))
        E("vector", lambda en: en.tensor_scalar_mul(out=acol[:], in0=sp_r[:, 0:256], scalar1=constp[:, 2:3]),
          reads=(T_SPR, "constp"), writes=("acol",))
        E("qsp", lambda en: en.dma_start(out=bbs[:], in_=bb_d), reads=(), writes=("bbs",), dkey="qbbs")
        E("qsp", lambda en: en.dma_start(out=cbs[:], in_=cb_d), reads=(), writes=("cbs",), dkey="qcbs")
        E("qsp", lambda en: en.dma_start(out=dtbs[:], in_=dtb_d), reads=(), writes=("dtbs",), dkey="qdtbs")
        E("qsp", lambda en: en.dma_start(out=dsks[:], in_=dsk_d), reads=(), writes=("dsks",), dkey="qdsks")
        E("qsp", lambda en: en.dma_start(out=identb[:], in_=identb_d), reads=(), writes=("identb",), dkey="qident")

        def emitA(tc):
            v = tc % 2
            t0 = tc * TC
            for i in range(2):
                E("qsp", (lambda i_=i, t0_=t0: lambda en: en.dma_start(
                    out=x_tm[i_][:], in_=x_d[t0_ + i_ * 128: t0_ + (i_ + 1) * 128, :]))(),
                  reads=(), writes=(f"x_tm{i}",), dkey=f"qx{i}")
            for i in range(NSL):
                xi = x_tm[i % 2]
                xn_i = xn_tm[i]
                E("scalar", (lambda xi_=xi, xn_=xn_i: lambda en: en.activation(
                    out=xn_[:], in_=xi_[:], func=AF.Identity, accum_out=sx[:]))(),
                  reads=(f"x_tm{i % 2}",), writes=(f"xn_tm{i}", "sx"))
                E("scalar", (lambda xi_=xi, xn_=xn_i: lambda en: en.activation(
                    out=xn_[:], in_=xi_[:], func=AF.Square, accum_out=sx2[:]))(),
                  reads=(f"x_tm{i % 2}",), writes=(f"xn_tm{i}", "sx2"))
                E("vector", lambda en: en.tensor_scalar_mul(out=mvec[:], in0=sx[:], scalar1=constp[:, 6:7]),
                  reads=("sx", "constp"), writes=("mvec",))
                E("vector", lambda en: en.tensor_mul(msq[:], mvec[:], mvec[:]),
                  reads=(), sync_reads=("mvec",), writes=("msq",))
                E("vector", lambda en: en.tensor_scalar(
                    out=var[:], in0=sx2[:], scalar1=constp[:, 6:7], scalar2=msq[:],
                    op0=AL.mult, op1=AL.subtract),
                  reads=("sx2", "constp"), sync_reads=("msq",), writes=("var",))
                E("scalar", lambda en: en.activation(out=lnv[:], in_=var[:], func=AF.Ln, bias=constp[:, 0:1]),
                  reads=("var", "constp"), writes=("lnv",))
                E("scalar", lambda en: en.activation(out=rstd[:], in_=lnv[:], func=AF.Exp, scale=constp[:, 1:2]),
                  reads=("lnv", "constp"), writes=("rstd",))
                E("vector", lambda en: en.tensor_scalar(
                    out=negmr[:], in0=mvec[:], scalar1=rstd[:], scalar2=constp[:, 2:3],
                    op0=AL.mult, op1=AL.mult),
                  reads=("rstd", "constp"), sync_reads=("mvec",), writes=("negmr",))
                E("scalar", (lambda xi_=xi, xn_=xn_i: lambda en: en.activation(
                    out=xn_[:], in_=xi_[:], func=AF.Identity, scale=rstd[:], bias=negmr[:]))(),
                  reads=(f"x_tm{i % 2}", "negmr"), sync_reads=("rstd",), writes=(f"xn_tm{i}",))
                if i + 2 < NSL:
                    E("qsp", (lambda i_=i, t0_=t0: lambda en: en.dma_start(
                        out=x_tm[(i_ + 2) % 2][:],
                        in_=x_d[t0_ + (i_ + 2) * 128: t0_ + (i_ + 3) * 128, :]))(),
                      reads=(), writes=(f"x_tm{(i + 2) % 2}",), dkey=f"qx{(i + 2) % 2}")
            for c in range(KC):
                for i in range(NSL):
                    E("tensor", (lambda c_=c, i_=i: lambda en: en.transpose(
                        out=PSBF[:, i_ * 128:(i_ + 1) * 128],
                        in_=xn_tm[i_][:, c_ * 128:(c_ + 1) * 128], identity=identb[:]))(),
                      reads=(f"xn_tm{i}", "identb"), writes=("psbf",))
                E("scalar", (lambda c_=c, v_=v: lambda en: en.copy(out=xn_fm[v_][c_][:], in_=PSBF[:]))(),
                  reads=("psbf",), writes=(f"xn_fm{v}_{c}",))

        def emitB(tc):
            v = tc % 2
            E("qpool", lambda en: en.dma_start(out=dtw_buf[0][:], in_=dtw_d[:, 0:1024]),
              reads=(), writes=("dtw_buf0",), dkey="qdtw0")
            for ec in range(KC):
                j = ec % 2
                if ec + 1 < KC:
                    jn = (ec + 1) % 2
                    E("qpool", (lambda ec_=ec, jn_=jn: lambda en: en.dma_start(
                        out=dtw_buf[jn_][:], in_=dtw_d[:, (ec_ + 1) * 1024:(ec_ + 2) * 1024]))(),
                      reads=(), writes=(f"dtw_buf{jn}",), dkey=f"qdtw{jn}")
                pdst, ptok = (PS0, "ps0") if ec % 2 == 0 else (PS3, "ps3")
                for kc in range(KC):
                    E("tensor", (lambda j_=j, kc_=kc, pd_=pdst: lambda en: en.matmul(
                        out=pd_[:], lhsT=dtw_buf[j_][:, kc_ * 128:(kc_ + 1) * 128],
                        rhs=xn_fm[v][kc_][:], start=(kc_ == 0), stop=(kc_ == KC - 1)))(),
                      reads=(f"dtw_buf{j}", f"xn_fm{v}_{kc}", ptok), writes=(ptok,))
                E("scalar", (lambda ec_=ec, pd_=pdst: lambda en: en.activation(
                    out=sp_r[:], in_=pd_[:], func=AF.Relu, bias=dtbs[:, ec_:ec_ + 1]))(),
                  reads=(ptok, "dtbs"), writes=(T_SPR,))
                E("scalar", (lambda ec_=ec, pd_=pdst: lambda en: en.activation(
                    out=sp_a[:], in_=pd_[:], func=AF.Abs, bias=dtbs[:, ec_:ec_ + 1]))(),
                  reads=(ptok, "dtbs"), writes=(T_SPA,))
                E("scalar", lambda en: en.activation(out=sp_e[:], in_=sp_a[:], func=AF.Exp, scale=constp[:, 2:3]),
                  reads=(T_SPA, "constp"), writes=(T_SPE,))
                E("scalar", lambda en: en.activation(out=sp_a[:], in_=sp_e[:], func=AF.Ln, bias=constp[:, 3:4]),
                  reads=(T_SPE, "constp"), writes=(T_SPA,))
                E("vector", (lambda ec_=ec, v_=v: lambda en: en.tensor_add(
                    dt_fm[v_][ec_][:], sp_r[:], sp_a[:]))(),
                  reads=(T_SPR, T_SPA), writes=(f"dt_fm{v}_{ec}",))
                E("vector", (lambda ec_=ec, v_=v: lambda en: en.tensor_mul(
                    ub[v_][ec_][:], dt_fm[v_][ec_][:], xn_fm[v_][ec_][:]))(),
                  reads=(f"dt_fm{v}_{ec}", f"xn_fm{v}_{ec}"), writes=(f"ub{v}_{ec}",))
            for kc in range(KC):
                E("tensor", (lambda kc_=kc, v_=v: lambda en: en.matmul(
                    out=PSB[:], lhsT=bwb[:, kc_ * 32:(kc_ + 1) * 32], rhs=xn_fm[v_][kc_][:],
                    start=(kc_ == 0), stop=(kc_ == KC - 1)))(),
                  reads=("bwb", f"xn_fm{v}_{kc}", "psb"), writes=("psb",))
            E("scalar", lambda en: en.activation(out=b_sb[:], in_=PSB[:], func=AF.Identity, bias=bbs[:]),
              reads=("psb", "bbs"), writes=("b_sb",))
            for kc in range(KC):
                E("tensor", (lambda kc_=kc, v_=v: lambda en: en.matmul(
                    out=PSC[:], lhsT=cwb[:, kc_ * 32:(kc_ + 1) * 32], rhs=xn_fm[v_][kc_][:],
                    start=(kc_ == 0), stop=(kc_ == KC - 1)))(),
                  reads=("cwb", f"xn_fm{v}_{kc}", "psc"), writes=("psc",))
            E("scalar", lambda en: en.activation(out=c_sb[:], in_=PSC[:], func=AF.Identity, bias=cbs[:]),
              reads=("psc", "cbs"), writes=("c_sb",))
            E("qpool", (lambda v_=v: lambda en: en.dma_start(out=bscr[v_], in_=b_sb[:]))(),
              reads=("b_sb",), writes=(f"bscr{v}",), dkey="qbscr")
            E("qpool", (lambda v_=v: lambda en: en.dma_start(out=cscr[v_], in_=c_sb[:]))(),
              reads=("c_sb",), writes=(f"cscr{v}",), dkey="qcscr")

        def emitByinit(tc):
            v = tc % 2
            for ec in range(KC):
                E("vector", (lambda ec_=ec, v_=v: lambda en: en.tensor_scalar_mul(
                    out=y_acc[ec_][:], in0=xn_fm[v_][ec_][:], scalar1=dsks[:, ec_:ec_ + 1]))(),
                  reads=(f"xn_fm{v}_{ec}", "dsks"), writes=(f"y_acc{ec}",))

        def emitC_deferred(tc):
            sch.defer = []
            emitC(tc)
            out = sch.defer
            sch.defer = None
            return out

        def emitC(tc):
            v = tc % 2

            def load_gu(hc_):
                j = hc_ % 2
                E("qpool", (lambda h_=hc_, j_=j: lambda en: en.dma_start(
                    out=wgbuf[j_][:], in_=wg8_d[h_ * 128:(h_ + 1) * 128, :]))(),
                  reads=(), writes=(f"wgbuf{j}",), dkey=f"qwg{j}")
                E("qpool", (lambda h_=hc_, j_=j: lambda en: en.dma_start(
                    out=wubuf[j_][:], in_=wu8_d[h_ * 128:(h_ + 1) * 128, :]))(),
                  reads=(), writes=(f"wubuf{j}",), dkey=f"qwu{j}")

            load_gu(0)
            for hcp_ in range(HC // 2):
                for sub in range(2):
                    hc = 2 * hcp_ + sub
                    if hc + 1 < HC:
                        load_gu(hc + 1)
                    gps, gtok = (PS1, "ps1") if sub == 0 else (PS0, "ps0")
                    ups, utok = (PS2, "ps2") if sub == 0 else (PS3, "ps3")
                    j = hc % 2
                    for kc in range(KC):
                        E("tensor", (lambda j_=j, kc_=kc, v_=v, g_=gps: lambda en: en.matmul(
                            out=g_[:], lhsT=wgbuf[j_][:, kc_ * 128:(kc_ + 1) * 128],
                            rhs=xn_fm[v_][kc_][:], start=(kc_ == 0), stop=(kc_ == KC - 1)))(),
                          reads=(f"wgbuf{j}", f"xn_fm{v}_{kc}", gtok), writes=(gtok,))
                    for kc in range(KC):
                        E("tensor", (lambda j_=j, kc_=kc, v_=v, u_=ups: lambda en: en.matmul(
                            out=u_[:], lhsT=wubuf[j_][:, kc_ * 128:(kc_ + 1) * 128],
                            rhs=xn_fm[v_][kc_][:], start=(kc_ == 0), stop=(kc_ == KC - 1)))(),
                          reads=(f"wubuf{j}", f"xn_fm{v}_{kc}", utok), writes=(utok,))
                for sub in range(2):
                    hc = 2 * hcp_ + sub
                    gps, gtok = (PS1, "ps1") if sub == 0 else (PS0, "ps0")
                    E("scalar", (lambda s_=sub, g_=gps: lambda en: en.activation(
                        out=sg_b[s_][:], in_=g_[:], func=AF.Silu, scale=constp[:, 4:5]))(),
                      reads=(gtok, "constp"), writes=(f"sg_b{sub}",))
                for sub in range(2):
                    hc = 2 * hcp_ + sub
                    ups, utok = (PS2, "ps2") if sub == 0 else (PS3, "ps3")
                    E("scalar", (lambda s_=sub, u_=ups: lambda en: en.copy(
                        out=up_b[s_][:], in_=u_[:]))(),
                      reads=(utok,), writes=(f"up_b{sub}",))
                for sub in range(2):
                    hc = 2 * hcp_ + sub
                    E("gpsimd", (lambda s_=sub, h_=hc: lambda en: en.tensor_mul(
                        h_act[h_][:], sg_b[s_][:], up_b[s_][:]))(),
                      reads=(f"sg_b{sub}", f"up_b{sub}"), writes=(f"h_act{hc}",))
            for c in range(KC):
                pdst, ptok = (PS1, "ps1") if c % 2 == 0 else (PS2, "ps2")
                for half in range(2):
                    j = (2 * c + half) % 2
                    E("qpool", (lambda c_=c, hf_=half, j_=j: lambda en: en.dma_start(
                        out=wdhalf[j_][:],
                        in_=wd8_d[c_ * 128:(c_ + 1) * 128, hf_ * 1536:(hf_ + 1) * 1536]))(),
                      reads=(), writes=(f"wdhalf{j}",), dkey=f"qwd{j}")
                for half in range(2):
                    j = (2 * c + half) % 2
                    for h12 in range(12):
                        h = half * 12 + h12
                        E("tensor", (lambda j_=j, h12_=h12, h_=h, pd_=pdst: lambda en: en.matmul(
                            out=pd_[:], lhsT=wdhalf[j_][:, h12_ * 128:(h12_ + 1) * 128],
                            rhs=h_act[h_][:], start=(h_ == 0), stop=(h_ == HC - 1)))(),
                          reads=(f"wdhalf{j}", f"h_act{h}", ptok), writes=(ptok,))
                E("scalar", (lambda c_=c, pd_=pdst, v_=v: lambda en: en.activation(
                    out=g_fm[v_][c_][:], in_=pd_[:], func=AF.Silu, scale=constp[:, 5:6]))(),
                  reads=(ptok, "constp"), writes=(f"g_fm{v}_{c}",))

        def emitD(tc, cdefer=()):
            cq = list(cdefer)
            ci = 0

            def pump(k):
                nonlocal ci
                for _ in range(k):
                    if ci < len(cq):
                        e_, f_, r_, w_, s_, d_ = cq[ci]
                        ci += 1
                        sch.emit(e_, f_, r_, w_, s_, d_)

            v = tc % 2
            t0e = tc * TC
            for i in range(2):
                E("qsp", (lambda i_=i, t0_=t0e: lambda en: en.dma_start(
                    out=x_re[i_][:], in_=x_d[t0_ + i_ * 128: t0_ + (i_ + 1) * 128, :]))(),
                  reads=(), writes=(f"x_re{i}",), dkey=f"qxr{i}")

            def bcast(np2):
                j2 = np2 % 2
                for half in range(2):
                    n_ = 2 * np2 + half
                    E("qsp", (lambda n2=n_, j_=j2, h_=half, v_=v: lambda en: en.dma_start(
                        out=btp[j_][:, h_, :], in_=bscr[v_][n2:n2 + 1, :].partition_broadcast(128)))(),
                      reads=(f"bscr{v}",), writes=(f"btp{j2}_{half}",), dkey=f"qbt{2 * j2 + half}")
                    E("qsp", (lambda n2=n_, j_=j2, h_=half, v_=v: lambda en: en.dma_start(
                        out=ctp[j_][:, h_, :], in_=cscr[v_][n2:n2 + 1, :].partition_broadcast(128)))(),
                      reads=(f"cscr{v}",), writes=(f"ctp{j2}_{half}",), dkey=f"qct{2 * j2 + half}")

            bcast(0)
            kctr = 0
            pairctr = 0
            for np_ in range(NST // 2):
                n0, n1 = 2 * np_, 2 * np_ + 1
                j2 = np_ % 2
                if np_ + 1 < NST // 2:
                    bcast(np_ + 1)
                for c in range(KC):
                    kp = pairctr % 3
                    hj = pairctr % 2
                    ap_ = pairctr % 2
                    pairctr += 1
                    for (n_, half) in ((n0, 0), (n1, 1)):
                        E("scalar", (lambda c_=c, n2=n_, a_=ap_, h_=half, v_=v: lambda en: en.activation(
                            out=apair[a_][:, h_, 1:TC + 1], in_=dt_fm[v_][c_][:], func=AF.Exp,
                            scale=acol[:, c_ * 32 + n2: c_ * 32 + n2 + 1]))(),
                          reads=(f"dt_fm{v}_{c}", "acol"), writes=(f"apair{ap_}_{half}",))
                    for (n_, half) in ((n0, 0), (n1, 1)):
                        slot = c * 32 + n_
                        E("scalar", (lambda k_=kp, h_=half, sl_=slot: lambda en: en.copy(
                            out=wpair[k_][:, h_, 0:1], in_=carry[:, sl_:sl_ + 1]))(),
                          reads=(f"carry_{slot}",), writes=(f"wpair{kp}_inj",))
                    E("vector", (lambda c_=c, j_=j2, k_=kp, v_=v: lambda en: en.tensor_mul(
                        wpair[k_][:, :, 1:TC + 1],
                        ub[v_][c_][:].rearrange("p (o t) -> p o t", o=1).broadcast_to([128, 2, TC]),
                        btp[j_][:]))(),
                      reads=(f"ub{v}_{c}", f"btp{j2}_0", f"btp{j2}_1"), writes=(f"wpair{kp}",))
                    E("vector", (lambda a_=ap_, kp_=kp: lambda en: en.tensor_tensor_scan(
                        out=hpair[kp_][:].rearrange("p a t -> p (a t)"),
                        data0=apair[a_][:].rearrange("p a t -> p (a t)"),
                        data1=wpair[kp_][:].rearrange("p a t -> p (a t)"),
                        initial=0.0, op0=AL.mult, op1=AL.add))(),
                      reads=(f"apair{ap_}_0", f"apair{ap_}_1", f"wpair{kp}", f"wpair{kp}_inj"),
                      writes=(f"hpair{kp}_0", f"hpair{kp}_1"))
                    if tc + 1 < NTC:
                        for (n_, half) in ((n0, 0), (n1, 1)):
                            slot = c * 32 + n_
                            E("gpsimd", (lambda kp_=kp, h_=half, sl_=slot: lambda en: en.tensor_copy(
                                out=carry[:, sl_:sl_ + 1], in_=hpair[kp_][:, h_, TC:TC + 1]))(),
                              reads=(f"hpair{kp}_{half}",), writes=(f"carry_{slot}",))
                    E("vector", (lambda j_=j2, kp_=kp, hj_=hj: lambda en: en.tensor_mul(
                        hcp[hj_][:], hpair[kp_][:, :, 1:TC + 1], ctp[j_][:]))(),
                      reads=(f"hpair{kp}_0", f"hpair{kp}_1", f"ctp{j2}_0", f"ctp{j2}_1"),
                      writes=(f"hcp{hj}",))
                    pump(6)
                    if np_ % 2 == 0:
                        E("vector", (lambda c_=c, hj_=hj: lambda en: en.tensor_add(
                            pairc[c_][:], hcp[hj_][:, 0, :], hcp[hj_][:, 1, :]))(),
                          reads=(f"hcp{hj}",), writes=(f"pairc{c}",))
                    else:
                        E("vector", (lambda hj_=hj: lambda en: en.tensor_add(
                            quadb[0][:], hcp[hj_][:, 0, :], hcp[hj_][:, 1, :]))(),
                          reads=(f"hcp{hj}",), writes=("quadb0",))
                        E("vector", (lambda c_=c: lambda en: en.tensor_add(
                            quadb[0][:], pairc[c_][:], quadb[0][:]))(),
                          reads=(f"pairc{c}", "quadb0"), writes=("quadb0",))
                        E("vector", (lambda c_=c: lambda en: en.tensor_add(
                            y_acc[c_][:], y_acc[c_][:], quadb[0][:]))(),
                          reads=(f"y_acc{c}", "quadb0"), writes=(f"y_acc{c}",))
            pump(10 ** 9)

        def emitE(tc):
            v = tc % 2
            t0 = tc * TC
            for c in range(KC):
                E("vector", (lambda c_=c, v_=v: lambda en: en.tensor_mul(
                    ub[v_][c_][:], y_acc[c_][:], g_fm[v_][c_][:]))(),
                  reads=(f"y_acc{c}", f"g_fm{v}_{c}"), writes=(f"ub{v}_{c}",))
            for i in range(NSL):
                jo = 0
                for ch in range(2):
                    pb, pbtok = (PSBF, "psbf") if (2 * i + ch) % 2 == 0 else (PSBF2, "psbf2")
                    for c4 in range(4):
                        c = ch * 4 + c4
                        E("tensor", (lambda c_=c, i_=i, c4_=c4, pb_=pb, v_=v: lambda en: en.transpose(
                            out=pb_[:, c4_ * 128:(c4_ + 1) * 128],
                            in_=ub[v_][c_][:, i_ * 128:(i_ + 1) * 128], identity=identb[:]))(),
                          reads=(f"ub{v}_{c}", "identb"), writes=(pbtok,))
                    E("vector", (lambda i_=i, ch_=ch, jo_=jo, pb_=pb: lambda en: en.tensor_add(
                        out_tm[jo_][:, ch_ * 512:(ch_ + 1) * 512], pb_[:],
                        x_re[i_ % 2][:, ch_ * 512:(ch_ + 1) * 512]))(),
                      reads=(pbtok, f"x_re{i % 2}"), writes=(f"out_tm{jo}",))
                E("qsp", (lambda i_=i, jo_=jo, t0_=t0: lambda en: en.dma_start(
                    out=out_d[t0_ + i_ * 128: t0_ + (i_ + 1) * 128, :], in_=out_tm[jo_][:]))(),
                  reads=(f"out_tm{jo}",), writes=(f"outdma{tc}_{i}",), dkey=f"qout{jo}")
                if i + 2 < NSL:
                    E("qsp", (lambda i_=i, t0_=t0: lambda en: en.dma_start(
                        out=x_re[(i_ + 2) % 2][:],
                        in_=x_d[t0_ + (i_ + 2) * 128: t0_ + (i_ + 3) * 128, :]))(),
                      reads=(), writes=(f"x_re{(i + 2) % 2}",), dkey=f"qxr{(i + 2) % 2}")

        emitA(0)
        emitB(0)
        emitByinit(0)
        for tc in range(NTC):
            if tc + 1 < NTC:
                emitA(tc + 1)
                emitB(tc + 1)
            emitD(tc, emitC_deferred(tc))
            emitE(tc)
            if tc + 1 < NTC:
                emitByinit(tc + 1)

        for (eng_, fn_, w_, skey_, inc_) in sch.recs:
            assert not (eng_ in ("qsp", "qpool", "qact") and skey_ == eng_), "DMA record missing dkey"
        for sk in sch.scount:
            sems[sk] = es.enter_context(nc.semaphore(("s_" + str(sk))[:30]))

        def build_body(body_engs):
            def run(eng_obj):
                for (eng, fn, waits, skey, inc) in sch.recs:
                    if eng not in body_engs:
                        continue
                    ws = sorted(waits.items(), key=lambda kv: str(kv[0]))
                    for de, dv in ws[:-1]:
                        eng_obj.wait_ge(sems[de], dv)
                    ins = fn(eng_obj)
                    if ws:
                        ins = ins._wait_ge(sems[ws[-1][0]], ws[-1][1])
                    ins.then_inc(sems[skey], inc)
            return run

        block.tensor(build_body(("tensor",)))
        block.vector(build_body(("vector",)))
        block.scalar(build_body(("scalar", "qact")))
        block.gpsimd(build_body(("gpsimd", "qpool")))
        out_keys = [k for k in sch.scount if str(k).startswith("qout")]

        def sync_body(eng_obj):
            build_body(("qsp",))(eng_obj)
            for k in out_keys:
                eng_obj.wait_ge(sems[k], sch.scount[k])
        block.sync(sync_body)

    return nc


_CACHE = {}


def _get_nc(debug=False):
    key = ("nc", debug)
    if key not in _CACHE:
        _CACHE[key] = build_program()
    return _CACHE[key]


def kernel(L0_attn_x, L0_attn_log_A, L0_attn_B_w, L0_attn_B_b, L0_attn_C_w,
           L0_attn_C_b, L0_attn_D_skip, L0_attn_dt_w, L0_attn_dt_b,
           L0_ffn_w_gate, L0_ffn_w_up, L0_ffn_w_down):
    nc = _get_nc()
    f32 = np.float32
    x = np.ascontiguousarray(np.asarray(L0_attn_x, dtype=f32))
    dt_w = np.ascontiguousarray(np.asarray(L0_attn_dt_w, dtype=f32))
    B_w = np.ascontiguousarray(np.asarray(L0_attn_B_w, dtype=f32))
    C_w = np.ascontiguousarray(np.asarray(L0_attn_C_w, dtype=f32))
    log_A = np.ascontiguousarray(np.asarray(L0_attn_log_A, dtype=f32))

    dtw = np.ascontiguousarray(
        dt_w.reshape(8, 128, 8, 128).transpose(3, 0, 2, 1)).reshape(128, 8192)
    bw = np.ascontiguousarray(B_w.reshape(32, 8, 128).transpose(2, 1, 0)).reshape(128, 256)
    cw = np.ascontiguousarray(C_w.reshape(32, 8, 128).transpose(2, 1, 0)).reshape(128, 256)
    logA = np.ascontiguousarray(log_A.reshape(8, 128, 32).transpose(1, 0, 2)).reshape(128, 256)
    bb = np.ascontiguousarray(np.asarray(L0_attn_B_b, dtype=f32).reshape(32, 1))
    cb = np.ascontiguousarray(np.asarray(L0_attn_C_b, dtype=f32).reshape(32, 1))
    dtb = np.ascontiguousarray(np.asarray(L0_attn_dt_b, dtype=f32).reshape(8, 128).T)
    dsk = np.ascontiguousarray(np.asarray(L0_attn_D_skip, dtype=f32).reshape(8, 128).T)

    def int8_view(w):
        w = np.ascontiguousarray(np.asarray(w, dtype=np.int32))
        return w.view(np.int8).reshape(w.shape[0], w.shape[1], 4)[:, :, 0]

    wg8 = np.ascontiguousarray(
        int8_view(L0_ffn_w_gate).reshape(24, 128, 8, 128).transpose(0, 3, 2, 1)).reshape(HID, D)
    wu8 = np.ascontiguousarray(
        int8_view(L0_ffn_w_up).reshape(24, 128, 8, 128).transpose(0, 3, 2, 1)).reshape(HID, D)
    wd8 = np.ascontiguousarray(
        int8_view(L0_ffn_w_down).reshape(8, 128, 24, 128).transpose(0, 3, 2, 1)).reshape(D, HID)
    identb = np.eye(128, dtype=ml_dtypes.bfloat16)

    common = dict(dtw=dtw, bw=bw, cw=cw, bb=bb, cb=cb, dtb=dtb, dsk=dsk,
                  logA=logA, wg8=wg8, wu8=wu8, wd8=wd8, identb_in=identb)
    in_maps = [dict(common, x=np.ascontiguousarray(x[k])) for k in range(8)]
    res = run_bass_kernel_spmd(nc, in_maps, core_ids=list(range(8)))
    out = np.stack([np.asarray(r["out"], dtype=f32) for r in res.results])
    return out
